# version 31
# baseline (speedup 1.0000x reference)
import os as _os
import numpy as np
import concourse.bass as bass
import concourse.mybir as mybir
from concourse.bass_utils import run_bass_kernel_spmd

F32 = mybir.dt.float32
BF16 = mybir.dt.bfloat16
AF = mybir.ActivationFunctionType
ALU = mybir.AluOpType
AXX = mybir.AxisListType.X

D = 1024
T = 2048
NH = 16
DH = 64
NI = 4096
NDC = 8
NTC = 16
M64C = 1.5 * 2.0**52
M32C = 1.5 * 2.0**23
EPS = 1e-5


def mkap(t_ap, offset_elems, dims):
    return bass.AP(t_ap.tensor, offset_elems, dims)


def r3_in(src2d):
    return src2d.rearrange("(c p) e -> p c e", p=128)


def r3_out(dst2d, c):
    return dst2d.rearrange("p (c e) -> p c e", c=c)


class Prog:
    def __init__(self, nc):
        self.nc = nc
        self.guards = []
        self.sem = {}
        self.cnt = {}
        self.lastwait = {}
        for name in ["eT", "eV", "eS", "eG"]:
            g = nc.semaphore(name)
            self.sem[name] = g.__enter__()
            self.guards.append(g)
            self.cnt[name] = 0

    def dsem(self, name):
        if name not in self.sem:
            g = self.nc.semaphore(name)
            self.sem[name] = g.__enter__()
            self.guards.append(g)
            self.cnt[name] = 0
        return name

    def inc(self, ins, s, n=1):
        ins.then_inc(self.sem[s], n)
        self.cnt[s] += n
        return self.cnt[s]

    def dma(self, eng, engname, stream, out, in_, **kw):
        s = self.dsem(stream)
        lw = self.lastwait.get((engname, s), 0)
        if (self.cnt[s] - lw) >= 8 * 16:
            self.wait(eng, engname, s, self.cnt[s])
        self.inc(eng.dma_start(out=out, in_=in_, **kw), s, 16)
        return self.cnt[s]

    def wait(self, eng, engname, s, val=None):
        if val is None:
            val = self.cnt[s]
        if val <= 0:
            return
        key = (engname, s)
        if self.lastwait.get(key, 0) < val:
            eng.wait_ge(self.sem[s], val)
            self.lastwait[key] = val


class _Cut(Exception):
    pass


def build_nc():
    try:
        return _build_nc_inner()
    except _Cut as e:
        nc, fin = e.args
        fin()
        return nc


def _build_nc_inner():
    KPHASE = int(_os.environ.get("KPHASE", "99"))
    nc = bass.Bass("TRN2", target_bir_lowering=False)
    TE, V, S, G, Y = nc.tensor, nc.vector, nc.scalar, nc.gpsimd, nc.sync

    xT = nc.dram_tensor("xT", [D, T], F32, kind="ExternalInput").ap()
    cT = nc.dram_tensor("cT", [D, 1], F32, kind="ExternalInput").ap()
    wqkvT = nc.dram_tensor("wqkvT", [D, 3 * D], F32, kind="ExternalInput").ap()
    wmod = nc.dram_tensor("wmod", [D, 6 * D], F32, kind="ExternalInput").ap()
    wgT = nc.dram_tensor("wgT", [D, NI], F32, kind="ExternalInput").ap()
    wuT = nc.dram_tensor("wuT", [D, NI], F32, kind="ExternalInput").ap()
    wdT = nc.dram_tensor("wdT", [NI, D], F32, kind="ExternalInput").ap()
    ones128b = nc.dram_tensor("ones128b", [128, 128], BF16, kind="ExternalInput").ap()
    bdiagb = nc.dram_tensor("bdiagb", [128, 128], BF16, kind="ExternalInput").ap()
    ones128f = nc.dram_tensor("ones128f", [128, 128], F32, kind="ExternalInput").ap()
    onescolf = nc.dram_tensor("onescolf", [1, 128], F32, kind="ExternalInput").ap()
    onecolbf = nc.dram_tensor("onecolbf", [128, 1], BF16, kind="ExternalInput").ap()
    identf = nc.dram_tensor("identf", [128, 128], F32, kind="ExternalInput").ap()
    cstv_d = nc.dram_tensor("cstv", [128, 4], F32, kind="ExternalInput").ap()
    x2tmp = nc.dram_tensor("x2tmp", [D, T], F32, kind="Internal").ap()
    h2sp = nc.dram_tensor("h2sp", [D, T], F32, kind="Internal").ap()
    terng = nc.dram_tensor("terng", [128, 256 * 128], BF16, kind="Internal").ap()
    ternu = nc.dram_tensor("ternu", [128, 256 * 128], BF16, kind="Internal").ap()
    ternd = nc.dram_tensor("ternd", [128, 256 * 128], BF16, kind="Internal").ap()
    C8h = nc.dram_tensor("C8h", [NH, T], BF16, kind="Internal").ap()
    outT = nc.dram_tensor("outT", [D, T], F32, kind="ExternalOutput").ap()

    p = Prog(nc)
    eT, eV, eS, eG = "eT", "eV", "eS", "eG"

    def alloc(name, shape, dtype, side="left"):
        g = nc.sbuf_tensor(name, shape, dtype, side=side)
        return g, g.__enter__()[:]

    blkg = nc.Block()
    blkg.__enter__()

    def _finish(tile, n, m):
        p.wait(Y, "Y", eV)
        p.wait(Y, "Y", eS)
        p.wait(Y, "Y", eT)
        p.wait(Y, "Y", eG)
        p.dma(G, "G", "dcut", outT[0:n, 0:m], tile)
        p.wait(G, "G", "dcut")
        blkg.__exit__(None, None, None)

    pg = []

    def palloc(name, shape, dtype, side="left"):
        g, t = alloc(name, shape, dtype, side)
        pg.append(g)
        return t

    onesb = palloc("onesb", [128, 128], BF16)
    bdiag = palloc("bdiag", [128, 128], BF16)
    onesf = palloc("onesf", [128, 128], F32)
    onecf = palloc("onecf", [1, 128], F32)
    onecb = palloc("onecb", [128, 1], BF16)
    ident = palloc("ident", [128, 128], F32)
    g1m = palloc("g1m", [128, 1], F32)
    modv = palloc("modv", [128, 48], F32)
    avec = palloc("avec", [128, 8], F32)
    bvec = palloc("bvec", [128, 8], F32)
    gav = palloc("gav", [128, 8], F32)
    gbv = palloc("gbv", [128, 8], F32)
    negC = palloc("negC", [128, 16], F32)
    Nq = palloc("Nq", [128, 8], F32)
    Nk = palloc("Nk", [128, 8], F32)
    mstat = palloc("mstat", [128, 8], F32)
    Max1c = palloc("Max1c", [128, 16], F32)
    Max2c = palloc("Max2c", [128, 16], F32)
    Mtok = palloc("Mtok", [128, 16], F32)
    smallw = palloc("smallw", [128, 16], F32)
    S16 = palloc("S16", [16, 128], F32)
    cstv = palloc("cstv_s", [128, 4], F32)
    hA = palloc("hA", [128, NDC * T], BF16)

    for dst, src in [(onesb, ones128b), (bdiag, bdiagb), (onesf, ones128f),
                     (onecf, onescolf), (onecb, onecolbf), (ident, identf),
                     (cstv, cstv_d)]:
        p.dma(Y, "Y", "dc", dst, src)

    p.wait(V, "V", eV)
    p.inc(V.memset(g1m, 1.0), eV)
    p.wait(V, "V", eV)
    p.wait(V, "V", eV)
    p.inc(V.tensor_scalar(g1m, g1m, 2.0**-30, 1.0, ALU.add, ALU.is_le), eV)

    g_modrow, modrow = alloc("modrow", [1, 6 * D], F32, "right")
    g_wmb0, wmb0 = alloc("wmb0", [128, 4096], F32, "right")
    g_wmb1, wmb1 = alloc("wmb1", [128, 4096], F32, "right")
    g_ccol, ccol = alloc("ccol", [128, 8], F32, "right")
    g_silc, silc = alloc("silc", [128, 8], F32, "right")
    wmb = [wmb0, wmb1]

    p.dma(Y, "Y", "dc", ccol, cT.rearrange("(c p) o -> p (c o)", p=128),
          allow_slow_non_contiguous=True)
    tok_cc = p.cnt["dc"]
    p.wait(S, "S", "dc", tok_cc)
    p.wait(S, "S", eS)
    p.inc(S.activation(silc, ccol, AF.Silu), eS)
    tok_silc = p.cnt[eS]


    g_xs0, xs0 = alloc("xs0", [128, T], F32, "right")
    g_xs1, xs1 = alloc("xs1", [128, T], F32, "right")
    g_xb0, xb0 = alloc("xb0", [128, T], BF16, "right")
    g_xb1, xb1 = alloc("xb1", [128, T], BF16, "right")
    g_xq0, xq0 = alloc("xq0", [128, T], BF16, "right")
    g_xq1, xq1 = alloc("xq1", [128, T], BF16, "right")
    g_Rb, Rb = alloc("Rb", [128, T], F32, "right")
    g_MRb, MRb = alloc("MRb", [128, T], F32, "right")
    g_lsc, lsc = alloc("lsc", [128, 3 * 512], F32, "right")
    rbp_tok = [0]
    g_tscr, tscr0 = alloc("tscr", [128, T], F32, "right")
    g_tscr1, tscr1 = alloc("tscr1", [128, T], F32, "right")
    tscr = [tscr0, tscr1]
    xs, xbt, xqt = [xs0, xs1], [xb0, xb1], [xq0, xq1]

    def emit_ln_stats(src_dram, S1ps, S2ps, first_wait=None):
        use_tok = {}
        for d in range(NDC):
            buf = d % 2
            if d == 0 and first_wait is not None:
                first_wait()
            if d >= 2:
                p.wait(Y, "Y", eV, use_tok[(d - 2, "v")])
                p.wait(Y, "Y", eS, use_tok[(d - 2, "s")])
            dxs = "dx%d" % buf
            tok_x = p.dma(Y, "Y", dxs, xs[buf],
                          src_dram[d * 128:(d + 1) * 128, :])
            p.wait(V, "V", dxs, tok_x)
            if d >= 2:
                p.wait(V, "V", eT, use_tok[(d - 2, "mm")])
                p.wait(S, "S", eT, use_tok[(d - 2, "mm")])
            p.wait(V, "V", eV)
            p.inc(V.tensor_copy(xbt[buf], xs[buf]), eV)
            use_tok[(d, "v")] = p.cnt[eV]
            p.wait(S, "S", dxs, tok_x)
            p.wait(S, "S", eS)
            p.inc(S.activation(xqt[buf], xs[buf], AF.Square), eS)
            use_tok[(d, "s")] = p.cnt[eS]
            p.wait(TE, "T", eV, use_tok[(d, "v")])
            p.wait(TE, "T", eS, use_tok[(d, "s")])
            for gq in range(4):
                sl = slice(gq * 512, (gq + 1) * 512)
                p.inc(TE.matmul(S1ps[:, sl], onesb, xbt[buf][:, sl],
                                start=(d == 0), stop=(d == NDC - 1)), eT)
                p.inc(TE.matmul(S2ps[:, sl], onesb, xqt[buf][:, sl],
                                start=(d == 0), stop=(d == NDC - 1)), eT)
            use_tok[(d, "mm")] = p.cnt[eT]
        return p.cnt[eT]

    def emit_ln_rb(S1ps, S2ps, Rbo, MRbo, tok_stats):
        rbp_tok[0] = 0
        mub = lsc[:, 0:512]
        vrb = lsc[:, 512:1024]
        tmb = lsc[:, 1024:1536]
        for gq in range(4):
            sl = slice(gq * 512, (gq + 1) * 512)
            p.wait(V, "V", eT, tok_stats)
            if gq > 0:
                p.wait(V, "V", eS)
                p.wait(V, "V", eV)
            p.wait(V, "V", eV)
            p.inc(V.tensor_scalar(mub, S1ps[:, sl], 1.0 / D, None, ALU.mult), eV)
            p.wait(V, "V", eV)
            p.inc(V.tensor_scalar(vrb, S2ps[:, sl], 1.0 / D, None, ALU.mult), eV)
            rbp_tok[0] = p.cnt[eV]
            p.wait(V, "V", eV)
            p.wait(V, "V", eV)
            p.inc(V.tensor_mul(MRbo[:, sl], mub, mub), eV)
            p.wait(V, "V", eV)
            p.wait(V, "V", eV)
            p.inc(V.tensor_sub(vrb, vrb, MRbo[:, sl]), eV)
            tvv = p.cnt[eV]
            p.wait(S, "S", eV, tvv)
            p.wait(S, "S", eS)
            p.inc(S.activation(tmb, vrb, AF.Sqrt, bias=cstv[:, 0:1]), eS)
            p.wait(V, "V", eS)
            p.wait(V, "V", eV)
            p.inc(V.reciprocal(Rbo[:, sl], tmb), eV)
            p.wait(V, "V", eV)
            p.wait(V, "V", eV)
            p.inc(V.tensor_mul(MRbo[:, sl], mub, Rbo[:, sl]), eV)
        return p.cnt[eV]

    def emit_ln_apply(src_dram, scale_vec, shift_vec, dest_tile, Rbo, MRbo,
                      tok_rb, dtoks=None):
        use_tok = {}
        stok = {}
        for d in range(NDC):
            buf = d % 2
            if d < 2:
                p.wait(Y, "Y", eV)
                p.wait(Y, "Y", eS)
            if d >= 2:
                p.wait(Y, "Y", eV, use_tok[d - 2])
            dxs = "dx%d" % buf
            tok_x = p.dma(Y, "Y", dxs, xs[buf],
                          src_dram[d * 128:(d + 1) * 128, :])
            p.wait(V, "V", dxs, tok_x)
            p.wait(V, "V", eV, tok_rb)
            if d >= 2:
                p.wait(V, "V", eS, stok[d - 2])
            p.wait(V, "V", eV)
            p.inc(V.tensor_mul(tscr[buf], xs[buf], Rbo), eV)
            use_tok[d] = p.cnt[eV]
            p.wait(V, "V", eV)
            p.inc(V.tensor_sub(tscr[buf], tscr[buf], MRbo), eV)
            tok_n = p.cnt[eV]
            p.wait(S, "S", eV, tok_n)
            p.wait(S, "S", eS)
            p.inc(S.activation(dest_tile(d), tscr[buf], AF.Identity,
                               scale=scale_vec[:, d:d + 1],
                               bias=shift_vec[:, d:d + 1]), eS)
            stok[d] = p.cnt[eS]
            if dtoks is not None:
                dtoks[d] = p.cnt[eS]
        return p.cnt[eS]

    with (nc.psum_tensor("S1psA", [128, T], F32) as s1t,
          nc.psum_tensor("S2psA", [128, T], F32) as s2t):
        tok_stats = emit_ln_stats(xT, s1t[:], s2t[:], None)
        tok_rb = emit_ln_rb(s1t[:], s2t[:], Rb, MRb, tok_stats)
    with nc.psum_tensor("modps", [1, 4096], F32) as mpt:
        modps = mpt[:]
        mm_tok = {}
        for half, (c0, cw) in enumerate([(0, 4096), (4096, 2048)]):
            for j in range(NDC):
                buf = j % 2
                kk0 = half * NDC + j
                if kk0 >= 2:
                    p.wait(G, "G", eT, mm_tok[kk0 - 2])
                dm = "dgm%d" % buf
                tok_w = p.dma(G, "G", dm, wmb[buf][:, 0:cw],
                              wmod[j * 128:(j + 1) * 128, c0:c0 + cw])
                p.wait(TE, "T", "dc")
                p.wait(TE, "T", dm, tok_w)
                p.wait(TE, "T", eS, tok_silc)
                if kk0 == 0:
                    p.wait(TE, "T", eV, rbp_tok[0])
                if half == 1 and j == 0:
                    p.wait(TE, "T", eV)
                for grp in range(cw // 512):
                    p.inc(TE.matmul(modps[0:1, grp * 512:(grp + 1) * 512],
                                    silc[:, j:j + 1], wmb[buf][:, grp * 512:(grp + 1) * 512],
                                    start=(j == 0), stop=(j == NDC - 1)), eT)
                mm_tok[half * NDC + j] = p.cnt[eT]
            p.wait(V, "V", eT)
            p.wait(V, "V", eV)
            p.inc(V.tensor_copy(modrow[0:1, c0:c0 + cw], modps[0:1, 0:cw]), eV)
    tok_modrow = p.cnt[eV]

    with nc.psum_tensor("mvps", [128, 48], F32) as mvt:
        mvps = mvt[:]
        p.wait(TE, "T", eV, tok_modrow)
        for k48 in range(48):
            p.inc(TE.transpose(mvps[:, k48:k48 + 1],
                               modrow[0:1, k48 * 128:(k48 + 1) * 128],
                               ident[0:1, 0:1]), eT)
        tok_mv = p.cnt[eT]
        p.wait(V, "V", eT, tok_mv)
        p.wait(V, "V", eV)
        p.inc(V.tensor_copy(modv, mvps), eV)
    tok_modv = p.cnt[eV]
    tok_mv_T = tok_mv

    p.wait(V, "V", eV, tok_modv)
    p.wait(V, "V", eV)
    p.inc(V.tensor_scalar(avec, modv[:, 8:16], 1.0, None, ALU.add), eV)
    p.wait(V, "V", eV)
    p.inc(V.tensor_scalar(bvec, modv[:, 32:40], 1.0, None, ALU.add), eV)
    for (dstv, col) in [(gav, 16), (gbv, 40)]:
        ga = modv[:, col:col + 8]
        tq = smallw[:, 0:8]
        uq = smallw[:, 8:16]
        p.wait(V, "V", eV)
        p.wait(V, "V", eV)
        p.inc(V.tensor_scalar(tq, ga, 0.25, None, ALU.max), eV)
        p.wait(V, "V", eV)
        p.inc(V.tensor_scalar(uq, ga, -0.25, None, ALU.min), eV)
        p.wait(V, "V", eV)
        p.inc(V.tensor_scalar(dstv, ga, 0.0, None, ALU.is_ge), eV)
        p.wait(V, "V", eV)
        p.wait(V, "V", eV)
        p.inc(V.tensor_sub(tq, tq, uq), eV)
        p.wait(V, "V", eV)
        p.wait(V, "V", eV)
        p.inc(V.tensor_mul(dstv, dstv, tq), eV)
        p.wait(V, "V", eV)
        p.wait(V, "V", eV)
        p.inc(V.tensor_add(dstv, dstv, uq), eV)
    tok_modvecs = p.cnt[eV]
    if KPHASE < 2:
        _finish(modv, 128, 48)
        return nc

    p.wait(V, "V", eV, tok_modvecs)
    lnap_tok = {}
    tok_h = emit_ln_apply(xT, avec, modv[:, 0:8],
                          lambda d: hA[:, d * T:(d + 1) * T], Rb, MRb, tok_rb,
                          dtoks=lnap_tok)
    if KPHASE < 3:
        _finish(hA[:, 0:2048], 128, 2048)
        return nc

    for g in [g_tscr1, g_tscr, g_lsc, g_MRb, g_Rb, g_xq1, g_xq0, g_xb1, g_xb0,
              g_xs1, g_xs0]:
        g.__exit__(None, None, None)
    for g in [g_silc, g_ccol, g_wmb1, g_wmb0, g_modrow]:
        g.__exit__(None, None, None)

    g_Qc, Qc = alloc("Qc", [128, NDC * T], BF16, "left")
    g_Kc, Kc = alloc("Kc", [128, NDC * T], BF16, "left")
    g_Vc, Vc = alloc("Vc", [128, NTC * 1040], BF16, "left")

    g_wc0, wc0 = alloc("wc0", [128, 1024], BF16, "left")
    g_wc1, wc1 = alloc("wc1", [128, 1024], BF16, "left")
    wcol = [wc0, wc1]

    ones_ap = mkap(Vc, 64, [[NTC * 1040, 128], [1040, NTC], [130, 8], [1, 2]])
    p.wait(V, "V", eV)
    p.inc(V.memset(ones_ap, 1.0), eV)
    tok_vones = p.cnt[eV]

    g_mb0, mb0 = alloc("mb0", [128, 4096], F32, "right")
    g_mb1, mb1 = alloc("mb1", [128, 4096], F32, "right")
    g_macc, macc = alloc("macc", [128, 24], F32, "right")
    mbuf = [mb0, mb1]
    mean_plan = []
    for (wsrc, base) in [(wgT, 0), (wuT, 8), (wdT, 16)]:
        for j in range(8):
            mean_plan.append((wsrc, base, j))
    mean_state = {"k": 0, "red": {}}

    def emit_mean_unit():
        k = mean_state["k"]
        if k >= len(mean_plan):
            return
        wsrc, base, j = mean_plan[k]
        red_tok = mean_state["red"]
        if k == 0:
            p.wait(Y, "Y", eS, tok_h)
            p.wait(Y, "Y", eV)
            p.wait(Y, "Y", eT)
        if wsrc.shape[0] == D:
            src = wsrc[j * 128:(j + 1) * 128, :]
        else:
            src = r3_in(wsrc[j * 512:(j + 1) * 512, :])
        buf = k % 2
        if k >= 2:
            sem_e, tok_e = red_tok[k - 2]
            p.wait(Y, "Y", sem_e, tok_e)
        outm = mbuf[buf] if wsrc.shape[0] == D else r3_out(mbuf[buf], 4)
        dns = "dn%d" % buf
        tok_m = p.dma(Y, "Y", dns, outm, src)
        dstc = macc[:, base + j:base + j + 1]
        p.wait(V, "V", dns, tok_m)
        p.wait(V, "V", eV)
        p.inc(V.reduce_sum(dstc, mbuf[buf], axis=AXX,
                           apply_absolute_value=True), eV)
        red_tok[k] = (eV, p.cnt[eV])
        mean_state["k"] = k + 1

    with (nc.psum_tensor("qkpsA", [128, T], F32) as qa,
          nc.psum_tensor("qkpsB", [128, T], F32) as qb):
        qkps = [qa[:], qb[:]]
        evac_tok = {0: (eV, 0), 1: (eV, 0)}
        mm_tok = {}
        for ec in range(16):
            buf = ec % 2
            if ec == 0:
                p.wait(G, "G", eV)
                p.wait(G, "G", eS)
                p.wait(G, "G", eT)
            if ec >= 2:
                p.wait(G, "G", eT, mm_tok[ec - 2])
            src = r3_in(wqkvT[:, ec * 128:(ec + 1) * 128])
            dqs = "dq%d" % buf
            tok_w = p.dma(G, "G", dqs, r3_out(wcol[buf], 8), src)
            p.wait(TE, "T", dqs, tok_w)
            sem_e, tok_e = evac_tok[buf]
            p.wait(TE, "T", sem_e, tok_e)
            for d in range(NDC):
                if ec == 0:
                    p.wait(TE, "T", eS, lnap_tok[d])
                for gq in range(4):
                    sl = slice(gq * 512, (gq + 1) * 512)
                    p.inc(TE.matmul(qkps[buf][:, sl],
                                    wcol[buf][:, d * 128:(d + 1) * 128],
                                    hA[:, d * T + gq * 512: d * T + (gq + 1) * 512],
                                    start=(d == 0), stop=(d == NDC - 1)), eT)
            mm_tok[ec] = p.cnt[eT]
            dst = (Qc if ec < 8 else Kc)[:, (ec % 8) * T:((ec % 8) + 1) * T]
            if ec % 2 == 0:
                p.wait(V, "V", eT, mm_tok[ec])
                p.wait(V, "V", eV)
                p.inc(V.tensor_copy(dst, qkps[buf]), eV)
                evac_tok[buf] = (eV, p.cnt[eV])
            else:
                p.wait(S, "S", eT, mm_tok[ec])
                p.wait(S, "S", eS)
                p.inc(S.activation(dst, qkps[buf], AF.Copy), eS)
                evac_tok[buf] = (eS, p.cnt[eS])
            emit_mean_unit()
        tok_qk_v = p.cnt[eV]
        tok_qk_s = p.cnt[eS]

    g_wv, wvt = alloc("wvt", [128, NDC * 1024], BF16, "right")
    src = r3_in(wqkvT[:, 2048:3072])
    tok_wv = p.dma(G, "G", "dv", r3_out(wvt, 8), src)
    with (nc.psum_tensor("vpsA", [128, 1024], F32) as va,
          nc.psum_tensor("vpsB", [128, 1024], F32) as vb):
        vps = [va[:], vb[:]]
        vevac = [0, 0]
        for t in range(NTC):
            buf = t % 2
            if t == 0:
                p.wait(TE, "T", eV, tok_qk_v)
                p.wait(TE, "T", eS, tok_qk_s)
            p.wait(TE, "T", "dv", tok_wv)
            p.wait(TE, "T", eV, vevac[buf])
            for d in range(NDC):
                lhsT = hA[:, d * T + t * 128: d * T + (t + 1) * 128]
                for n2 in range(2):
                    p.inc(TE.matmul(vps[buf][:, n2 * 512:(n2 + 1) * 512], lhsT,
                                    wvt[:, d * 1024 + n2 * 512: d * 1024 + (n2 + 1) * 512],
                                    start=(d == 0), stop=(d == NDC - 1)), eT)
            tok_vmm = p.cnt[eT]
            dst = mkap(Vc, t * 1040, [[NTC * 1040, 128], [130, 8], [66, 2], [1, 64]])
            p.wait(V, "V", eT, tok_vmm)
            p.wait(V, "V", eV, tok_vones)
            p.wait(V, "V", eV)
            p.inc(V.tensor_copy(dst, vps[buf]), eV)
            vevac[buf] = p.cnt[eV]
            if t % 2 == 0:
                emit_mean_unit()
    tok_v = p.cnt[eV]
    g_wv.__exit__(None, None, None)
    while mean_state["k"] < len(mean_plan):
        emit_mean_unit()

    p.wait(V, "V", eG)
    p.wait(V, "V", eV)
    with nc.psum_tensor("mredps", [128, 4], F32) as mrt:
        mred = mrt[:]
        for mi in range(3):
            p.wait(V, "V", eV)
            p.inc(V.reduce_sum(smallw[:, mi:mi + 1], macc[:, mi * 8:mi * 8 + 8],
                               axis=AXX), eV)
        tok_colsum = p.cnt[eV]
        p.wait(TE, "T", eV, tok_colsum)
        for mi in range(3):
            p.inc(TE.matmul(mred[:, mi:mi + 1], onesf, smallw[:, mi:mi + 1],
                            start=True, stop=True), eT)
        tok_mred = p.cnt[eT]
        p.wait(V, "V", eT, tok_mred)
        for mi in range(3):
            p.wait(V, "V", eV)
            p.inc(V.tensor_scalar(mstat[:, 2 * mi:2 * mi + 1], mred[:, mi:mi + 1],
                                  1.0 / (NI * D), 1e-5, ALU.mult, ALU.max), eV)
        p.wait(V, "V", eV)
        for mi in range(3):
            p.wait(V, "V", eV)
            p.inc(V.reciprocal(mstat[:, 2 * mi + 1:2 * mi + 2],
                               mstat[:, 2 * mi:2 * mi + 1]), eV)
    tok_mstat = p.cnt[eV]
    for g in [g_macc, g_mb1, g_mb0]:
        g.__exit__(None, None, None)

    g_sqb, sqb = alloc("sqb", [128, T], BF16, "right")
    g_sqq, sqq = alloc("sqq", [128, T], BF16, "right")
    g_qnr, qnr = alloc("qnr", [128, T], F32, "right")
    g_c8s, c8s = alloc("c8s", [128, T], BF16, "right")
    with (nc.psum_tensor("nps", [128, T], F32) as npt,
          nc.psum_tensor("npsq", [128, T], F32) as npqt):
        nps = npt[:]
        npsq = npqt[:]
        kmax_tok = {}
        kmm_tok = {}
        qmm_tok = {}
        negk_tok = {}
        c8v_tok = {}
        c8d_tok = {}

        def emit_knorm(c):
            p.wait(S, "S", eV, tok_mstat)
            p.wait(S, "S", eV, tok_qk_v)
            p.wait(S, "S", eS, tok_qk_s)
            if c >= 1:
                p.wait(S, "S", eT, kmm_tok[c - 1])
            p.wait(S, "S", eS)
            p.inc(S.activation(sqb, Kc[:, c * T:(c + 1) * T], AF.Square), eS)
            tok_sq = p.cnt[eS]
            p.wait(TE, "T", eS, tok_sq)
            if c >= 1:
                p.wait(TE, "T", eV, kmax_tok[c - 1])
            for gq in range(4):
                sl = slice(gq * 512, (gq + 1) * 512)
                p.inc(TE.matmul(nps[:, sl], bdiag, sqb[:, sl],
                                start=True, stop=True), eT)
            kmm_tok[c] = p.cnt[eT]
            p.wait(V, "V", eT, kmm_tok[c])
            p.wait(V, "V", eV)
            p.inc(V.reduce_max(Nk[:, c:c + 1], nps, axis=AXX), eV)
            kmax_tok[c] = p.cnt[eV]
            p.wait(S, "S", eV, kmax_tok[c])
            p.wait(S, "S", eS)
            p.inc(S.activation(Nq[:, c:c + 1], Nk[:, c:c + 1], AF.Sqrt), eS)
            tok_sn = p.cnt[eS]
            p.wait(V, "V", eS, tok_sn)
            p.wait(V, "V", eV)
            p.inc(V.tensor_scalar(Nq[:, c:c + 1], Nq[:, c:c + 1], -1.0,
                                  None, ALU.mult), eV)
            negk_tok[c] = p.cnt[eV]

        def emit_qnorm(c):
            p.wait(S, "S", eV, negk_tok[c])
            if c >= 1:
                p.wait(S, "S", eT, qmm_tok[c - 1])
            p.wait(S, "S", eS)
            p.inc(S.activation(sqq, Qc[:, c * T:(c + 1) * T], AF.Square), eS)
            tok_sq = p.cnt[eS]
            p.wait(TE, "T", eS, tok_sq)
            if c >= 1:
                p.wait(TE, "T", eS)
            for gq in range(4):
                sl = slice(gq * 512, (gq + 1) * 512)
                p.inc(TE.matmul(npsq[:, sl], bdiag, sqq[:, sl],
                                start=True, stop=True), eT)
            qmm_tok[c] = p.cnt[eT]
            p.wait(S, "S", eT, qmm_tok[c])
            if c >= 1:
                p.wait(S, "S", eV, c8v_tok[c - 1])
            p.wait(S, "S", eS)
            p.inc(S.activation(qnr, npsq, AF.Sqrt), eS)
            tok_qn = p.cnt[eS]
            p.wait(V, "V", eS, tok_qn)
            if c >= 1:
                p.wait(V, "V", "dc8", c8d_tok[c - 1])
            p.wait(V, "V", eV)
            p.inc(V.tensor_scalar(c8s, qnr, Nq[:, c:c + 1], None, ALU.mult), eV)
            c8v_tok[c] = p.cnt[eV]
            for par in range(2):
                hh = 2 * c + par
                rsl = slice(64 * par, 64 * par + 1)
                p.wait(Y, "Y", eV, c8v_tok[c])
                p.dma(Y, "Y", "dc8", C8h[hh:hh + 1, :], c8s[rsl, :])
            c8d_tok[c] = p.cnt["dc8"]

        for c in range(NDC):
            emit_knorm(c)
            if c >= 1:
                emit_qnorm(c - 1)
        emit_qnorm(NDC - 1)
    tok_c8 = p.cnt["dc8"]
    tok_negC = p.cnt[eV]
    tok_negC_T = p.cnt[eT]
    tok_p2_s = p.cnt[eS]
    if KPHASE < 4:
        _finish(Qc[:, 0:2048], 128, 2048)
        return nc
    for g in [g_c8s, g_qnr, g_sqq, g_sqb, g_wc1, g_wc0]:
        g.__exit__(None, None, None)

    g_drow, drow = alloc("drow", [1, 512], F32, "right")
    g_bcsb, bcsb = alloc("bcsb", [128, 512], BF16, "right")
    g_c80, c80 = alloc("c80", [1, 1024], BF16, "right")
    g_tw0, tw0 = alloc("tw0", [128, 256], F32, "right")
    g_tw1, tw1 = alloc("tw1", [128, 256], F32, "right")
    g_tw2, tw2 = alloc("tw2", [128, 256], F32, "right")
    g_tw3, tw3 = alloc("tw3", [128, 256], F32, "right")
    g_tta, tta = alloc("tta", [128, 256], BF16, "right")
    g_ttb, ttb = alloc("ttb", [128, 256], BF16, "right")
    g_tto0, tto0 = alloc("tto0", [128, 256], BF16, "right")
    g_tto1, tto1 = alloc("tto1", [128, 256], BF16, "right")
    g_tto2, tto2 = alloc("tto2", [128, 256], BF16, "right")
    g_tto3, tto3 = alloc("tto3", [128, 256], BF16, "right")
    twb = [tw0, tw1, tw2, tw3]
    ttо = [tto0, tto1, tto2, tto3]

    PW = 256
    tern_plan = []
    for (wsrc_, tdst_, rmc_, ssem_, nb_) in [
            (wgT, terng, mstat[:, 1:2], "dwg", 8),
            (wuT, ternu, mstat[:, 3:4], "dwu", 8),
            (wdT, ternd, mstat[:, 5:6], "dwd", 32)]:
        for r_ in range(wsrc_.shape[0] // 128):
            for cc_ in range(wsrc_.shape[1] // PW):
                base_ = (2 * cc_ * nb_ + r_) * 128
                tern_plan.append((wsrc_, tdst_, rmc_, ssem_, r_, cc_, nb_, base_))
    tern_state = {"kk": 0, "gtok": {}, "stok": {}}

    def emit_tern_piece():
        kk = tern_state["kk"]
        if kk >= len(tern_plan):
            return
        gtok = tern_state["gtok"]
        wsrc_, tdst_, rmc_, ssem_, r_, cc_, nb_, base_ = tern_plan[kk]
        buf = kk % 4
        dts = "dt%d" % buf
        if kk < 4:
            p.wait(G, "G", eV, tok_mstat)
            p.wait(G, "G", eV, tok_negC)
            p.wait(G, "G", eT, tok_negC_T)
            p.wait(G, "G", eS, tok_p2_s)
        sname = ssem_ + str(buf)
        if kk >= 4:
            p.wait(G, "G", eV, gtok[kk - 4])
        tok_w = p.dma(G, "G", dts, twb[buf],
                      wsrc_[r_ * 128:(r_ + 1) * 128, cc_ * PW:(cc_ + 1) * PW])
        p.wait(V, "V", dts, tok_w)
        p.wait(V, "V", "dc8", tok_c8)
        p.wait(V, "V", eV, tok_mstat)
        p.wait(V, "V", eV)
        p.inc(V.tensor_scalar(tta, twb[buf], rmc_, 0.5, ALU.mult, ALU.is_gt), eV)
        p.inc(V.tensor_scalar(ttb, twb[buf], rmc_, -0.5, ALU.mult, ALU.is_lt), eV)
        if kk >= 4:
            ps, pt = tern_state["stok"][kk - 4]
            p.wait(V, "V", ps, pt)
        p.wait(V, "V", eV)
        p.inc(V.tensor_sub(ttо[buf], tta, ttb), eV)
        gtok[kk] = p.cnt[eV]
        p.wait(Y, "Y", eV, gtok[kk])
        dst3 = mkap(tdst_, base_, [[256 * 128, 128], [nb_ * 128, 2], [1, 128]])
        p.dma(Y, "Y", sname, dst3,
              ttо[buf].rearrange("p (j e) -> p j e", j=2))
        tern_state["stok"][kk] = (sname, p.cnt[sname])
        tern_state["kk"] = kk + 1

    def emit_tern_n(n):
        for _ in range(n):
            emit_tern_piece()


    g_PT0, PT0 = alloc("PT0", [128, NTC * 1024], BF16, "left")
    g_PT1, PT1 = alloc("PT1", [128, NTC * 1024], BF16, "left")
    PT = [PT0, PT1]

    with (nc.psum_tensor("sps0", [128, 1024], F32) as s0t,
          nc.psum_tensor("sps1", [128, 1024], F32) as s1t_,
          nc.psum_tensor("avps0", [128, 512], F32) as av0t,
          nc.psum_tensor("avps1", [128, 512], F32) as av1t,
          nc.psum_tensor("dops", [1, 512], F32) as dot,
          nc.psum_tensor("bcps", [128, 512], F32) as bct):
        sps = [s0t[:], s1t_[:]]
        avps = [av0t[:], av1t[:]]
        dops = dot[:]
        bcps = bct[:]
        exp_sbuf_tok = [0, 0]
        pt_avdone = [0, 0]
        av_evac_tok = [0, 0]
        do_read_tok = 0
        bc_read_tok = 0
        drow_read = 0

        p.wait(TE, "T", eV, tok_v)
        p.wait(TE, "T", eV, tok_qk_v)
        p.wait(TE, "T", eV, tok_negC)
        p.wait(TE, "T", eS, tok_qk_s)
        c8tok = {}
        p.wait(Y, "Y", "dc8", tok_c8)
        c8row_tok = {(0, 0): p.dma(Y, "Y", "dc80", c80, C8h[0:1, 0:1024])}
        for h in range(NH):
            c = h // 2
            par = h % 2
            bp = 64 * par
            cb = c80
            for half in range(2):
                hb = half
                tok_c8row = c8row_tok[(h, half)]
                for kt in range(NTC):
                    sp = kt % 2
                    p.wait(TE, "T", eS, exp_sbuf_tok[sp])
                    if kt == 0:
                        p.wait(TE, "T", "dc80", tok_c8row)
                    lhsT = Kc[bp:bp + 64, c * T + kt * 128: c * T + (kt + 1) * 128]
                    for n2 in range(2):
                        cs = n2 * 512
                        p.inc(TE.matmul(sps[sp][:, n2 * 512:(n2 + 1) * 512],
                                        onesb[0:1, :], cb[0:1, cs:cs + 512],
                                        start=True, stop=False), eT)
                    if kt == NTC - 1:
                        tok_lastr1 = p.cnt[eT]
                    for n2 in range(2):
                        q0 = c * T + half * 1024 + n2 * 512
                        p.inc(TE.matmul(sps[sp][:, n2 * 512:(n2 + 1) * 512],
                                        lhsT, Qc[bp:bp + 64, q0:q0 + 512],
                                        start=False, stop=True,
                                        tile_position=(bp, 0)), eT)
                    tok_sc = p.cnt[eT]
                    p.wait(S, "S", eT, tok_sc)
                    if kt == 0:
                        p.wait(S, "S", eT, pt_avdone[hb])
                    p.wait(S, "S", eS)
                    p.inc(S.activation(PT[hb][:, kt * 1024:(kt + 1) * 1024],
                                       sps[sp], AF.Exp, scale=0.125), eS)
                    exp_sbuf_tok[sp] = p.cnt[eS]
                    tok_lastsc = p.cnt[eT]
                    if 2 <= kt <= 13:
                        emit_tern_piece()
                tok_exp_all = p.cnt[eS]
                nh, nhalf = (h, 1) if half == 0 else (h + 1, 0)
                if nh < NH:
                    p.wait(Y, "Y", eT, tok_lastr1)
                    c8row_tok[(nh, nhalf)] = p.dma(
                        Y, "Y", "dc80", c80,
                        C8h[nh:nh + 1, nhalf * 1024:(nhalf + 1) * 1024])
                for gg in range(2):
                    gq = half * 2 + gg
                    avb = gg
                    p.wait(TE, "T", eS, tok_exp_all)
                    p.wait(TE, "T", eV, av_evac_tok[avb])
                    if par == 1:
                        p.wait(TE, "T", eS, do_read_tok)
                    for kt in range(NTC):
                        rhs = PT[hb][:, kt * 1024 + gg * 512: kt * 1024 + (gg + 1) * 512]
                        base = kt * 1040 + c * 130
                        if par == 0:
                            p.inc(TE.matmul(avps[avb][0:65, :],
                                            Vc[:, base:base + 65], rhs,
                                            start=(kt == 0), stop=(kt == NTC - 1)), eT)
                        else:
                            p.inc(TE.matmul(avps[avb][64:128, :],
                                            Vc[:, base + 66:base + 130], rhs,
                                            start=(kt == 0), stop=(kt == NTC - 1),
                                            tile_position=(0, 64)), eT)
                            p.inc(TE.matmul(dops[0:1, :], onecb, rhs,
                                            start=(kt == 0), stop=(kt == NTC - 1)), eT)
                    tok_av = p.cnt[eT]
                    if gg == 1:
                        pt_avdone[hb] = tok_av
                    if half == 1 and gg == 1:
                        c8tok[h] = p.cnt[eT]
                    p.wait(S, "S", eT, tok_av)
                    p.wait(S, "S", eT, drow_read)
                    p.wait(S, "S", eS)
                    if par == 0:
                        p.inc(S.activation(drow, avps[avb][64:65, :], AF.Copy), eS)
                    else:
                        p.inc(S.activation(drow, dops[0:1, :], AF.Copy), eS)
                        do_read_tok = p.cnt[eS]
                    tok_dr = p.cnt[eS]
                    p.wait(TE, "T", eS, tok_dr)
                    p.wait(TE, "T", eV, bc_read_tok)
                    p.inc(TE.matmul(bcps[:], onecf, drow, start=True, stop=True), eT)
                    tok_bc = p.cnt[eT]
                    p.wait(V, "V", eT, tok_bc)
                    p.wait(V, "V", eV)
                    with nc.allow_low_precision(
                            reason="fp32 reciprocal, single RNE round to bf16;"
                                   " identical to prior fp32-recip+copy"):
                        p.inc(V.reciprocal(bcsb, bcps[:]), eV)
                    bc_read_tok = p.cnt[eV]
                    drow_read = tok_bc
                    dst = hA[bp:bp + 64, c * T + gq * 512: c * T + (gq + 1) * 512]
                    srcav = avps[avb][0:64, :] if par == 0 else avps[avb][64:128, :]
                    bcs = bcsb[0:64, :] if par == 0 else bcsb[64:128, :]
                    p.wait(V, "V", eV)
                    p.inc(V.tensor_tensor(dst, srcav, bcs, ALU.mult), eV)
                    av_evac_tok[avb] = p.cnt[eV]
        tok_attn = p.cnt[eV]
        tok_p3T = p.cnt[eT]

    emit_tern_n(len(tern_plan))
    tern_streams = [m + str(b) for m in ("dwg", "dwu", "dwd") for b in range(4)
                    if (m + str(b)) in p.cnt]
    tok_terns = {s_: p.cnt[s_] for s_ in tern_streams}
    for g in [g_tto3, g_tto2, g_tto1, g_tto0, g_ttb, g_tta,
              g_tw3, g_tw2, g_tw1, g_tw0,
              g_c80, g_bcsb, g_drow, g_PT1, g_PT0, g_Vc, g_Kc, g_Qc]:
        g.__exit__(None, None, None)
    if KPHASE < 7:
        _finish(hA[:, 0:2048], 128, 2048)
        return nc

    g_alb, alphab = alloc("alphab", [128, T], F32, "right")
    g_mabs, mabs = alloc("mabs", [128, T], F32, "right")
    g_S16b, S16b = alloc("S16b", [16, 128], F32, "right")
    g_xs0b, xs0b = alloc("xs0b", [128, T], F32, "right")
    g_xs1b, xs1b = alloc("xs1b", [128, T], F32, "right")
    g_x20, x20 = alloc("x20", [128, T], F32, "right")
    g_x21, x21 = alloc("x21", [128, T], F32, "right")
    g_xbb0, xbb0 = alloc("xbb0", [128, T], BF16, "right")
    g_xbb1, xbb1 = alloc("xbb1", [128, T], BF16, "right")
    g_xqq0, xqq0 = alloc("xqq0", [128, T], BF16, "right")
    g_xqq1, xqq1 = alloc("xqq1", [128, T], BF16, "right")
    g_Rb2, Rb2 = alloc("Rb2", [128, T], F32, "right")
    g_MRb2, MRb2 = alloc("MRb2", [128, T], F32, "right")
    g_lsc2, lsc2 = alloc("lsc2", [128, 3 * 512], F32, "right")
    g_h2s, h2s = alloc("h2s", [128, T], F32, "right")
    g_h2s1, h2s1 = alloc("h2s1", [128, T], F32, "right")
    g_habs, habs = alloc("habs", [128, T], F32, "right")
    h2sb = [h2s, h2s1]
    xsb = [xs0b, xs1b]
    x2b = [x20, x21]
    xbb = [xbb0, xbb1]
    xqq = [xqq0, xqq1]
    sbv = modv[:, 24:32]

    for engh, engn in [(Y, "Y"), (V, "V"), (S, "S")]:
        for s_ in tern_streams + ["dc80"]:
            if s_ in p.cnt:
                p.wait(engh, engn, s_, p.cnt[s_])
        p.wait(engh, engn, eG)
        p.wait(engh, engn, eT, tok_p3T)
    with (nc.psum_tensor("S1ps2", [128, T], F32) as s1t,
          nc.psum_tensor("S2ps2", [128, T], F32) as s2t):
        S1ps, S2ps = s1t[:], s2t[:]
        use_tok = {}
        for d in range(NDC):
            buf = d % 2
            dxs = "dgx%d" % buf
            if d == 0:
                p.wait(G, "G", eG)
                p.wait(G, "G", eT, tok_p3T)
            if d >= 2:
                p.wait(G, "G", eV, use_tok[(d - 2, "x2stt")])
                p.wait(G, "G", eS, use_tok[(d - 2, "x2s")])
                p.wait(V, "V", "dsx%d" % buf, use_tok[(d - 2, "st")])
                p.wait(V, "V", eS, use_tok[(d - 2, "x2s")])
            tok_xd = p.dma(G, "G", dxs, xsb[buf], xT[d * 128:(d + 1) * 128, :])
            p.wait(V, "V", dxs, tok_xd)
            p.wait(V, "V", eV, tok_attn)
            p.wait(V, "V", eV)
            p.inc(V.scalar_tensor_tensor(x2b[buf], hA[:, d * T:(d + 1) * T],
                                         gav[:, d:d + 1], xsb[buf],
                                         ALU.mult, ALU.add), eV)
            tok_x2 = p.cnt[eV]
            use_tok[(d, "x2stt")] = tok_x2
            p.wait(Y, "Y", eV, tok_x2)
            use_tok[(d, "st")] = p.dma(Y, "Y", "dsx%d" % buf,
                                       x2tmp[d * 128:(d + 1) * 128, :], x2b[buf])
            p.wait(V, "V", eV, tok_x2)
            if d >= 2:
                p.wait(V, "V", eT, use_tok[(d - 2, "mm")])
                p.wait(S, "S", eT, use_tok[(d - 2, "mm")])
            p.wait(V, "V", eV)
            p.inc(V.tensor_copy(xbb[buf], x2b[buf]), eV)
            use_tok[(d, "x2v")] = p.cnt[eV]
            p.wait(S, "S", eV, tok_x2)
            p.wait(S, "S", eS)
            p.inc(S.activation(xqq[buf], x2b[buf], AF.Square), eS)
            use_tok[(d, "x2s")] = p.cnt[eS]
            p.wait(TE, "T", eV, use_tok[(d, "x2v")])
            p.wait(TE, "T", eS, use_tok[(d, "x2s")])
            for gq in range(4):
                sl = slice(gq * 512, (gq + 1) * 512)
                p.inc(TE.matmul(S1ps[:, sl], onesb, xbb[buf][:, sl],
                                start=(d == 0), stop=(d == NDC - 1)), eT)
                p.inc(TE.matmul(S2ps[:, sl], onesb, xqq[buf][:, sl],
                                start=(d == 0), stop=(d == NDC - 1)), eT)
            use_tok[(d, "mm")] = p.cnt[eT]
        tok_stats2 = p.cnt[eT]
        mub = lsc2[:, 0:512]
        vrb = lsc2[:, 512:1024]
        tmb = lsc2[:, 1024:1536]
        for gq in range(4):
            sl = slice(gq * 512, (gq + 1) * 512)
            p.wait(V, "V", eT, tok_stats2)
            if gq > 0:
                p.wait(V, "V", eS)
                p.wait(V, "V", eV)
            p.wait(V, "V", eV)
            p.inc(V.tensor_scalar(mub, S1ps[:, sl], 1.0 / D, None, ALU.mult), eV)
            p.wait(V, "V", eV)
            p.inc(V.tensor_scalar(vrb, S2ps[:, sl], 1.0 / D, None, ALU.mult), eV)
            p.wait(V, "V", eV)
            p.wait(V, "V", eV)
            p.inc(V.tensor_mul(MRb2[:, sl], mub, mub), eV)
            p.wait(V, "V", eV)
            p.wait(V, "V", eV)
            p.inc(V.tensor_sub(vrb, vrb, MRb2[:, sl]), eV)
            tvv = p.cnt[eV]
            p.wait(S, "S", eV, tvv)
            p.wait(S, "S", eS)
            p.inc(S.activation(tmb, vrb, AF.Sqrt, bias=cstv[:, 0:1]), eS)
            p.wait(V, "V", eS)
            p.wait(V, "V", eV)
            p.inc(V.reciprocal(Rb2[:, sl], tmb), eV)
            p.wait(V, "V", eV)
            p.wait(V, "V", eV)
            p.inc(V.tensor_mul(MRb2[:, sl], mub, Rb2[:, sl]), eV)
        tok_rb2 = p.cnt[eV]
        if KPHASE == 61:
            raise _Cut(nc, lambda: _finish(hA[:, 0:2048], 128, 2048))

    h2tok = {}

    def emit_h2_pass(consume, hread):
        p.wait(Y, "Y", "dsx0")
        p.wait(Y, "Y", "dsx1")
        use2 = {}
        for d in range(NDC):
            buf = d % 2
            dxs = "dx%d" % buf
            if d < 2:
                p.wait(Y, "Y", eV)
                p.wait(Y, "Y", eS)
            if d >= 2:
                p.wait(Y, "Y", eV, use2[d - 2])
            tok_xd = p.dma(Y, "Y", dxs, xsb[buf],
                           x2tmp[d * 128:(d + 1) * 128, :])
            p.wait(V, "V", dxs, tok_xd)
            p.wait(V, "V", eV, tok_rb2)
            if d >= 2:
                for sm_, tk_ in hread(d - 2):
                    p.wait(V, "V", sm_, tk_)
                p.wait(V, "V", eS, h2tok[d - 2])
            p.wait(V, "V", eV)
            p.inc(V.tensor_mul(x2b[buf], xsb[buf], Rb2), eV)
            use2[d] = p.cnt[eV]
            p.wait(V, "V", eV)
            p.inc(V.tensor_sub(x2b[buf], x2b[buf], MRb2), eV)
            tok_x2n = p.cnt[eV]
            p.wait(S, "S", eV, tok_x2n)
            if d >= 2:
                for sm_, tk_ in hread(d - 2):
                    p.wait(S, "S", sm_, tk_)
            p.wait(S, "S", eS)
            p.inc(S.activation(h2sb[buf], x2b[buf], AF.Identity,
                               scale=bvec[:, d:d + 1], bias=sbv[:, d:d + 1]), eS)
            h2tok[d] = p.cnt[eS]
            consume(d)
        return

    abs_tok = {}
    max_tok = {}

    def emit_maxA(d_):
        p.wait(V, "V", eS, abs_tok[d_])
        p.wait(V, "V", eV)
        if d_ == 0:
            p.inc(V.tensor_copy(mabs, habs), eV)
        else:
            p.inc(V.tensor_max(mabs, mabs, habs), eV)
        max_tok[d_] = p.cnt[eV]

    sp_tok = {}

    def consumeA(d):
        if d >= 1:
            emit_maxA(d - 1)
        p.wait(Y, "Y", eS, h2tok[d])
        sp_tok[d] = p.dma(Y, "Y", "dh2%d" % (d % 2),
                          h2sp[d * 128:(d + 1) * 128, :], h2sb[d % 2])
        p.wait(S, "S", eS, h2tok[d])
        if d >= 1:
            p.wait(S, "S", eV, max_tok[d - 1])
        p.wait(S, "S", eS)
        p.inc(S.activation(habs, h2sb[d % 2], AF.Abs), eS)
        abs_tok[d] = p.cnt[eS]

    emit_h2_pass(consumeA, lambda d_: [(eS, abs_tok[d_]),
                                       ("dh2%d" % (d_ % 2), sp_tok[d_])])
    emit_maxA(NDC - 1)
    tok_mabs = p.cnt[eV]
    if KPHASE == 62:
        raise _Cut(nc, lambda: _finish(mabs, 128, 2048))

    def max_dance(mabs_ap, dst_cols, tok_ready, tag, nblk, mtok_cols,
                  pingpong=True):
        import contextlib
        ctx = contextlib.ExitStack()
        tpa = ctx.enter_context(nc.psum_tensor("tpsA" + tag, [128, 128], F32))
        if pingpong:
            tpb = ctx.enter_context(nc.psum_tensor("tpsB" + tag, [128, 128], F32))
            tp2 = [tpa[:], tpb[:]]
        else:
            tp2 = [tpa[:], tpa[:]]
        with ctx:
            red2 = {}
            for b in range(nblk):
                buf = b % 2
                p.wait(TE, "T", eV, tok_ready)
                gd = b - 2 if pingpong else b - 1
                if gd >= 0:
                    p.wait(TE, "T", eV, red2[gd])
                p.inc(TE.transpose(tp2[buf], mabs_ap[:, b * 128:(b + 1) * 128],
                                   ident), eT)
                tok_tp = p.cnt[eT]
                p.wait(V, "V", eT, tok_tp)
                p.wait(V, "V", eV)
                p.inc(V.reduce_max(mtok_cols[:, b:b + 1], tp2[buf], axis=AXX), eV)
                red2[b] = p.cnt[eV]
        p.wait(V, "V", eV)
        p.wait(V, "V", eV)
        p.inc(V.tensor_scalar(dst_cols, mtok_cols, 1e-5, None, ALU.max), eV)
        return p.cnt[eV]

    def dance_to_s16(tok_src_cols, nb, tok_ready, tag, dst=None):
        if dst is None:
            dst = S16
        with nc.psum_tensor("dtp" + tag, [min(nb, 128), 128], F32) as dtp:
            p.wait(TE, "T", eV, tok_ready)
            p.inc(TE.transpose(dtp[:], tok_src_cols, ident), eT)
            tok_t = p.cnt[eT]
            p.wait(V, "V", eT, tok_t)
            p.wait(V, "V", eV)
            p.inc(V.tensor_copy(dst[0:nb, :], dtp[:]), eV)
        return p.cnt[eV]

    def bcast_from_s16(out_psum, nb, tok_s16v, src=None):
        if src is None:
            src = S16
        p.wait(TE, "T", eV, tok_s16v)
        for b in range(nb):
            lhsT = ident[0:16, b:b + 1].broadcast_to([16, 128])
            p.inc(TE.matmul(out_psum[:, b * 128:(b + 1) * 128], lhsT,
                            src[0:16, :], start=True, stop=True), eT)
        return p.cnt[eT]

    tok_m1 = max_dance(mabs, Max1c, tok_mabs, "1", NTC, Mtok)
    p.wait(V, "V", eV, tok_m1)
    p.wait(V, "V", eV)
    p.inc(V.reciprocal(smallw, Max1c), eV)
    p.wait(V, "V", eV)
    p.wait(V, "V", eV)
    p.inc(V.tensor_scalar(smallw, smallw, 127.0, None, ALU.mult), eV)
    tok_s1tok = p.cnt[eV]
    tok_s1s16 = dance_to_s16(smallw, 16, tok_s1tok, "s1")
    if KPHASE == 63:
        raise _Cut(nc, lambda: _finish(S16[0:16, :], 16, 128))

    p.wait(V, "V", eV, tok_mstat)
    p.wait(V, "V", eV)
    p.inc(V.tensor_scalar(smallw, Max1c, 1.0 / 127.0, None, ALU.mult), eV)
    p.wait(V, "V", eV)
    p.inc(V.tensor_scalar(smallw, smallw, mstat[:, 0:1], None, ALU.mult), eV)
    tok_al2 = p.cnt[eV]
    tok_als16 = dance_to_s16(smallw, 16, tok_al2, "al", dst=S16b)
    with nc.psum_tensor("albps", [128, T], F32) as alt:
        alps = alt[:]
        tok_alps = bcast_from_s16(alps, 16, tok_als16, src=S16b)
        p.wait(V, "V", eT, tok_alps)
        p.wait(V, "V", eV)
        p.inc(V.tensor_copy(alphab, alps[:]), eV)
    tok_alphab = p.cnt[eV]

    g_gpt = nc.psum_tensor("gps", [128, 2 * 512], F32)
    gpt = g_gpt.__enter__()
    g_upt = nc.psum_tensor("ups", [128, 2 * 512], F32)
    upt = g_upt.__enter__()

    g_vq, vq = alloc("vq", [128, T], F32, "right")
    g_aq, aq = alloc("aq", [128, T], F32, "right")
    tok_h2q_d = {}
    with nc.psum_tensor("sbps", [128, T], F32) as sbt:
        Sbps = sbt[:]
        tok_sb = bcast_from_s16(Sbps, 16, tok_s1s16)

        vv_tok = {}
        bb_tok = {}

        def emit_sttB(d_):
            p.wait(V, "V", eS, bb_tok[d_])
            p.wait(V, "V", eV, tok_attn)
            p.wait(V, "V", eV)
            p.inc(V.scalar_tensor_tensor(hA[:, d_ * T:(d_ + 1) * T], habs,
                                         g1m[:, 0:1], aq, ALU.mult, ALU.add), eV)
            tok_h2q_d[d_] = p.cnt[eV]

        for d in range(NDC):
            buf = d % 2
            dxs = "dx%d" % buf
            if d < 2:
                p.wait(Y, "Y", eV)
                p.wait(Y, "Y", eS)
            p.wait(Y, "Y", "dh2%d" % buf, sp_tok[d])
            if d >= 2:
                p.wait(Y, "Y", eV, vv_tok[d - 2])
            tok_hd = p.dma(Y, "Y", dxs, xsb[buf],
                           h2sp[d * 128:(d + 1) * 128, :])
            if d >= 1:
                emit_sttB(d - 1)
            p.wait(V, "V", eT, tok_sb)
            p.wait(V, "V", dxs, tok_hd)
            if d >= 1:
                p.wait(V, "V", eS, bb_tok[d - 1])
            p.wait(V, "V", eV)
            p.inc(V.tensor_tensor(vq, xsb[buf], Sbps, ALU.mult), eV)
            vv_tok[d] = p.cnt[eV]
            p.wait(V, "V", eV)
            p.inc(V.tensor_scalar(aq, vq, M64C, -M64C, ALU.add, ALU.add), eV)
            p.wait(S, "S", eV, vv_tok[d])
            if d >= 1:
                p.wait(S, "S", eV, tok_h2q_d[d - 1])
            p.wait(S, "S", eS)
            p.inc(S.add(habs, vq, cstv[:, 1:2]), eS)
            p.wait(S, "S", eS)
            p.inc(S.add(habs, habs, cstv[:, 2:3]), eS)
            bb_tok[d] = p.cnt[eS]
        emit_sttB(NDC - 1)
        tok_h2q = p.cnt[eV]
        tok_pBs = p.cnt[eS]
        if KPHASE == 64:
            raise _Cut(nc, lambda: _finish(hA[:, 0:2048], 128, 2048))
    for g in [g_aq, g_vq]:
        g.__exit__(None, None, None)

    for g in [g_habs, g_h2s1, g_h2s, g_lsc2, g_MRb2, g_Rb2, g_xqq1, g_xqq0,
              g_xbb1, g_xbb0, g_x21, g_x20, g_xs1b, g_xs0b, g_S16b, g_mabs]:
        g.__exit__(None, None, None)
    if KPHASE < 8:
        _finish(hA[:, 0:2048], 128, 2048)
        return nc

    g_gup, gup = alloc("gup", [128, 32 * 512], F32, "right")
    g_gq, gqb = alloc("gq", [128, 32 * 512], BF16, "right")
    g_s2bt, s2bt = alloc("s2bt", [128, 512], F32, "right")
    g_gmbt, gmbt = alloc("gmbt", [128, 512], F32, "right")
    g_mab2, mab2 = alloc("mab2", [128, 512], F32, "right")

    g_tg0, tg0 = alloc("tg0", [128, 1024], BF16, "left")
    g_tg1, tg1 = alloc("tg1", [128, 1024], BF16, "left")
    g_tg2, tg2 = alloc("tg2", [128, 1024], BF16, "left")
    g_tu0, tu0 = alloc("tu0", [128, 1024], BF16, "left")
    g_tu1, tu1 = alloc("tu1", [128, 1024], BF16, "left")
    g_tu2, tu2 = alloc("tu2", [128, 1024], BF16, "left")
    g_gpb, gpb = alloc("gpb", [128, 512], F32, "right")
    g_sgb, sgb = alloc("sgb", [128, 512], F32, "right")
    g_vq2, vq2 = alloc("vq2", [128, 2 * 512], F32, "right")
    g_aq2, aq2 = alloc("aq2", [128, 2 * 512], BF16, "right")
    g_bq2, bq2 = alloc("bq2", [128, 2 * 512], F32, "right")
    vq2b = [vq2[:, 0:512], vq2[:, 512:1024]]
    aq2b = [aq2[:, 0:512], aq2[:, 512:1024]]
    bq2b = [bq2[:, 0:512], bq2[:, 512:1024]]
    g_td0, td0 = alloc("td0", [128, 4096], BF16, "right")
    g_td1, td1 = alloc("td1", [128, 4096], BF16, "right")
    g_td2, td2 = alloc("td2", [128, 4096], BF16, "right")
    g_x2g0, x2g0 = alloc("x2g0", [128, 512], F32, "right")
    g_x2g1, x2g1 = alloc("x2g1", [128, 512], F32, "right")
    g_ob0, ob0 = alloc("ob0", [128, 512], F32, "right")
    g_ob1, ob1 = alloc("ob1", [128, 512], F32, "right")
    tgc = [tg0, tg1, tg2]
    tuc = [tu0, tu1, tu2]
    tdc = [td0, td1, td2]
    x2g = [x2g0, x2g1]
    outb = [ob0, ob1]


    with (nc.psum_tensor("bc2", [128, 512], F32) as bc2t,
          nc.psum_tensor("dps", [128, 2 * 512], F32) as dpt):
        gps = [gpt[:][:, 0:512], gpt[:][:, 512:1024]]
        ups = [upt[:][:, 0:512], upt[:][:, 512:1024]]
        bc2 = bc2t[:]
        dps = [dpt[:][:, 0:512], dpt[:][:, 512:1024]]
        wread_tok = {}
        gevac = {}
        uevac = {}
        devac = {}
        qv_tok, qa_tok, qb_tok, qs_tok = {}, {}, {}, {}
        bc2_read = 0
        for tg in range(4):
            ts0 = tg * 512
            for oc in range(32):
                buf = oc % 3
                kk = tg * 32 + oc
                dfs = "dgf%d" % buf
                if kk == 0:
                    p.wait(G, "G", eV, tok_alphab)
                if kk >= 3:
                    p.wait(G, "G", eT, wread_tok[kk - 3])
                for s_ in tern_streams:
                    if s_[2] in "gu":
                        p.wait(G, "G", s_, tok_terns[s_])
                p.dma(G, "G", dfs, tgc[buf], terng[:, oc * 1024:(oc + 1) * 1024])
                tok_wd = p.dma(G, "G", dfs, tuc[buf],
                               ternu[:, oc * 1024:(oc + 1) * 1024])
                p.wait(TE, "T", dfs, tok_wd)
                sem_t, tok_t = gevac.get((tg, oc - 2, "g"), (eV, 0))
                p.wait(TE, "T", sem_t, tok_t)
                sem_t, tok_t = uevac.get((tg, oc - 2, "u"), (eV, 0))
                p.wait(TE, "T", sem_t, tok_t)
                pb = oc % 2
                for d in range(NDC):
                    if tg == 0 and oc == 0:
                        p.wait(TE, "T", eV, tok_h2q_d[d])
                    p.inc(TE.matmul(gps[pb], tgc[buf][:, d * 128:(d + 1) * 128],
                                    hA[:, d * T + ts0: d * T + ts0 + 512],
                                    start=(d == 0), stop=(d == NDC - 1)), eT)
                for d in range(NDC):
                    p.inc(TE.matmul(ups[pb], tuc[buf][:, d * 128:(d + 1) * 128],
                                    hA[:, d * T + ts0: d * T + ts0 + 512],
                                    start=(d == 0), stop=(d == NDC - 1)), eT)
                tok_gu = p.cnt[eT]
                wread_tok[kk] = tok_gu
                p.wait(V, "V", eT, tok_gu)
                p.wait(V, "V", eV, tok_alphab)
                if kk > 0:
                    p.wait(V, "V", eS)
                p.wait(V, "V", eV)
                p.inc(V.tensor_tensor(gpb, gps[pb], alphab[:, ts0:ts0 + 512],
                                      ALU.mult), eV)
                tok_gp = p.cnt[eV]
                gevac[(tg, oc, "g")] = (eV, tok_gp)
                p.wait(S, "S", eV, tok_gp)
                if kk > 0:
                    p.wait(S, "S", eV)
                p.wait(S, "S", eS)
                p.inc(S.activation(sgb, gpb, AF.Silu), eS)
                tok_sg = p.cnt[eS]
                p.wait(V, "V", eS, tok_sg)
                gslice = gup[:, oc * 512:(oc + 1) * 512]
                p.wait(V, "V", eV)
                p.inc(V.tensor_tensor(gslice, sgb, ups[pb], ALU.mult), eV)
                tok_gm = p.cnt[eV]
                uevac[(tg, oc, "u")] = (eV, tok_gm)
                p.wait(S, "S", eV, tok_gm)
                p.wait(S, "S", eS)
                p.inc(S.activation(sgb, gslice, AF.Abs), eS)
                tok_ab = p.cnt[eS]
                p.wait(V, "V", eS, tok_ab)
                if oc == 0:
                    p.wait(V, "V", eV)
                    p.inc(V.tensor_copy(mab2, sgb), eV)
                else:
                    p.wait(V, "V", eV)
                    p.inc(V.tensor_max(mab2, mab2, sgb), eV)
            tok_guptg = p.cnt[eV]
            tok_m2 = max_dance(mab2, Max2c[:, tg * 4:(tg + 1) * 4], tok_guptg,
                               "2_%d" % tg, 4, Mtok[:, tg * 4:(tg + 1) * 4],
                               pingpong=False)
            p.wait(V, "V", eV, tok_m2)
            p.wait(V, "V", eV)
            p.inc(V.reciprocal(smallw[:, 0:4], Max2c[:, tg * 4:(tg + 1) * 4]), eV)
            p.wait(V, "V", eV)
            p.wait(V, "V", eV)
            p.inc(V.tensor_scalar(smallw[:, 0:4], smallw[:, 0:4], 127.0,
                                  None, ALU.mult), eV)
            p.wait(V, "V", eV)
            p.inc(V.tensor_mul(smallw[:, 4:8], Max1c[:, tg * 4:(tg + 1) * 4],
                               Max2c[:, tg * 4:(tg + 1) * 4]), eV)
            p.wait(V, "V", eV)
            p.wait(V, "V", eV)
            p.inc(V.tensor_scalar(smallw[:, 4:8], smallw[:, 4:8],
                                  mstat[:, 2:3], None, ALU.mult), eV)
            p.wait(V, "V", eV)
            p.wait(V, "V", eV)
            p.inc(V.tensor_scalar(smallw[:, 4:8], smallw[:, 4:8],
                                  mstat[:, 4:5], None, ALU.mult), eV)
            p.wait(V, "V", eV)
            p.wait(V, "V", eV)
            p.inc(V.tensor_scalar(smallw[:, 4:8], smallw[:, 4:8],
                                  1.0 / (127.0 * 127.0), None, ALU.mult), eV)
            tok_sg2 = p.cnt[eV]
            tok_d16 = dance_to_s16(smallw[:, 0:8], 8, tok_sg2, "sg%d" % tg)
            p.wait(TE, "T", eV, bc2_read)
            p.wait(TE, "T", eV, tok_d16)
            for bb in range(4):
                lhsT = ident[0:16, bb:bb + 1].broadcast_to([16, 128])
                p.inc(TE.matmul(bc2[:, bb * 128:(bb + 1) * 128],
                                lhsT, S16[0:16, :], start=True, stop=True), eT)
            tok_b1 = p.cnt[eT]
            p.wait(V, "V", eT, tok_b1)
            p.wait(V, "V", eV)
            p.inc(V.tensor_copy(s2bt, bc2), eV)
            bc2_read = p.cnt[eV]
            p.wait(TE, "T", eV, bc2_read)
            for bb in range(4):
                lhsT = ident[0:16, 4 + bb:5 + bb].broadcast_to([16, 128])
                p.inc(TE.matmul(bc2[:, bb * 128:(bb + 1) * 128], lhsT,
                                S16[0:16, :], start=True, stop=True), eT)
            tok_b2 = p.cnt[eT]
            p.wait(V, "V", eT, tok_b2)
            p.wait(V, "V", eV)
            p.inc(V.tensor_copy(gmbt, bc2), eV)
            bc2_read = p.cnt[eV]
            tok_scales = p.cnt[eV]
            dtoks = {}
            for d_ in (0, 1, 2):
                kkd = tg * 8 + d_
                if tg == 0:
                    p.wait(G, "G", eV, tok_h2q)
                    p.wait(G, "G", eS, tok_pBs)
                if kkd >= 3:
                    p.wait(G, "G", eT, devac.get((kkd - 3, "mm"), 0))
                for s_ in tern_streams:
                    if s_[2] == "d":
                        p.wait(G, "G", s_, tok_terns[s_])
                dtoks[d_] = p.dma(G, "G", "dd%d" % (d_ % 3), tdc[d_ % 3],
                                  ternd[:, d_ * 4096:(d_ + 1) * 4096])

            def emit_qstt(oc_):
                b2_ = oc_ % 2
                qk_ = tg * 32 + oc_
                p.wait(V, "V", eS, qb_tok[qk_])
                p.wait(V, "V", eV)
                p.inc(V.scalar_tensor_tensor(gqb[:, oc_ * 512:(oc_ + 1) * 512],
                                             bq2b[b2_], g1m[:, 0:1], aq2b[b2_],
                                             ALU.mult, ALU.add), eV)
                qs_tok[qk_] = p.cnt[eV]

            def emit_dmm(oc_):
                for d_ in (0, 1):
                    kkd = tg * 8 + d_
                    if oc_ == 0:
                        p.wait(TE, "T", "dd%d" % d_, dtoks[d_])
                        if kkd >= 2:
                            p.wait(TE, "T", eV, devac.get((kkd - 2, "ev"), 0))
                    p.wait(TE, "T", eV, qs_tok[tg * 32 + oc_])
                    p.inc(TE.matmul(dps[d_], tdc[d_][:, oc_ * 128:(oc_ + 1) * 128],
                                    gqb[:, oc_ * 512:(oc_ + 1) * 512],
                                    start=(oc_ == 0), stop=(oc_ == 31)), eT)
                if oc_ == 31:
                    for d_ in (0, 1):
                        devac[(tg * 8 + d_, "mm")] = p.cnt[eT]
            for oc in range(32):
                b2 = oc % 2
                qk = tg * 32 + oc
                gsl = gup[:, oc * 512:(oc + 1) * 512]
                p.wait(V, "V", eV, tok_scales)
                if qk >= 2:
                    p.wait(V, "V", eS, qb_tok[qk - 2])
                p.wait(V, "V", eV)
                p.inc(V.tensor_tensor(vq2b[b2], gsl, s2bt, ALU.mult), eV)
                qv_tok[qk] = p.cnt[eV]
                p.wait(V, "V", eV)
                p.inc(V.tensor_scalar(aq2b[b2], vq2b[b2], M64C, -M64C,
                                      ALU.add, ALU.add), eV)
                qa_tok[qk] = p.cnt[eV]
                p.wait(S, "S", eV, qv_tok[qk])
                if qk >= 2:
                    p.wait(S, "S", eV, qs_tok[qk - 2])
                p.wait(S, "S", eS)
                p.inc(S.add(bq2b[b2], vq2b[b2], cstv[:, 1:2]), eS)
                p.wait(S, "S", eS)
                p.inc(S.add(bq2b[b2], bq2b[b2], cstv[:, 2:3]), eS)
                qb_tok[qk] = p.cnt[eS]
                if oc >= 1:
                    emit_qstt(oc - 1)
                    emit_dmm(oc - 1)
            emit_qstt(31)
            emit_dmm(31)
            tok_gqtg = p.cnt[eV]
            for d in range(NDC):
                buf = d % 2
                wb = d % 3
                kk = tg * 8 + d
                dxr = "dxr%d" % buf
                if d >= 3:
                    dds = "dd%d" % wb
                    p.wait(G, "G", eT, devac.get((kk - 3, "mm"), 0))
                    tok_wdd = p.dma(G, "G", dds, tdc[wb],
                                    ternd[:, d * 4096:(d + 1) * 4096])
                else:
                    tok_wdd = dtoks.get(d, 0)
                if tg == 0 and d == 0:
                    p.wait(Y, "Y", eV, tok_h2q)
                    p.wait(Y, "Y", eS, tok_pBs)
                if kk >= 2:
                    p.wait(Y, "Y", eV, devac.get((kk - 2, "ev"), 0))
                tok_x2d = p.dma(Y, "Y", dxr, x2g[buf],
                                x2tmp[d * 128:(d + 1) * 128, ts0:ts0 + 512])
                if d >= 2:
                    p.wait(TE, "T", "dd%d" % wb, tok_wdd)
                    p.wait(TE, "T", eV, tok_gqtg)
                    p.wait(TE, "T", eV, devac.get((kk - 2, "ev"), 0))
                    for oc in range(32):
                        p.inc(TE.matmul(dps[buf],
                                        tdc[wb][:, oc * 128:(oc + 1) * 128],
                                        gqb[:, oc * 512:(oc + 1) * 512],
                                        start=(oc == 0), stop=(oc == 31)), eT)
                    devac[(kk, "mm")] = p.cnt[eT]
                tok_dmm = devac[(kk, "mm")]
                p.wait(V, "V", eT, tok_dmm)
                p.wait(V, "V", dxr, tok_x2d)
                p.wait(V, "V", eV)
                if kk >= 2:
                    p.wait(V, "V", "dso%d" % buf, devac.get((kk - 2, "st"), 0))
                p.wait(V, "V", eV)
                p.inc(V.tensor_tensor(outb[buf], dps[buf], gmbt, ALU.mult), eV)
                p.wait(V, "V", eV)
                p.wait(V, "V", eV)
                p.inc(V.scalar_tensor_tensor(outb[buf], outb[buf], gbv[:, d:d + 1],
                                             x2g[buf], ALU.mult, ALU.add), eV)
                devac[(kk, "ev")] = p.cnt[eV]
                p.wait(Y, "Y", eV, devac[(kk, "ev")])
                devac[(kk, "st")] = p.dma(
                    Y, "Y", "dso%d" % buf,
                    outT[d * 128:(d + 1) * 128, ts0:ts0 + 512], outb[buf])
    g_upt.__exit__(None, None, None)
    g_gpt.__exit__(None, None, None)
    p.wait(Y, "Y", "dso0")
    p.wait(Y, "Y", "dso1")

    for g in [g_ob1, g_ob0, g_x2g1, g_x2g0, g_td2, g_td1, g_td0, g_bq2, g_aq2, g_vq2,
              g_sgb, g_gpb, g_tu2, g_tu1, g_tu0, g_tg2, g_tg1, g_tg0, g_mab2,
              g_gmbt, g_s2bt, g_gq, g_gup, g_alb]:
        g.__exit__(None, None, None)

    for g in reversed(pg):
        g.__exit__(None, None, None)
    blkg.__exit__(None, None, None)
    for g in reversed(p.guards):
        g.__exit__(None, None, None)
    return nc


_cache = {}


def kernel(L0_attn_z, L0_attn_W_qkv, L0_ffn_w_gate, L0_ffn_w_up, L0_ffn_w_down,
           L0_c, L0_wmod):
    import ml_dtypes
    B = L0_attn_z.shape[0]
    Wq = np.asarray(L0_attn_W_qkv, np.float32).reshape(NH, 3, DH, D)
    Wre = np.concatenate([Wq[:, 0].reshape(D, D), Wq[:, 1].reshape(D, D),
                          Wq[:, 2].reshape(D, D)], axis=0)
    wqkvT = np.ascontiguousarray(Wre.T)
    wgT = np.ascontiguousarray(np.asarray(L0_ffn_w_gate, np.float32).T)
    wuT = np.ascontiguousarray(np.asarray(L0_ffn_w_up, np.float32).T)
    wdT = np.ascontiguousarray(np.asarray(L0_ffn_w_down, np.float32).T)
    wmodc = np.ascontiguousarray(np.asarray(L0_wmod, np.float32))
    ones128b = np.ones((128, 128), ml_dtypes.bfloat16)
    bd = np.zeros((128, 128), ml_dtypes.bfloat16)
    bd[:64, :64] = 1
    bd[64:, 64:] = 1
    ones128f = np.ones((128, 128), np.float32)
    onescolf = np.ones((1, 128), np.float32)
    onecolbf = np.ones((128, 1), ml_dtypes.bfloat16)
    identf = np.eye(128, dtype=np.float32)
    cstv = np.tile(np.array([[EPS, M32C, -M32C, 0.0]], np.float32), (128, 1))

    if "nc" not in _cache:
        _cache["nc"] = build_nc()
    nc = _cache["nc"]

    in_maps = []
    for b in range(B):
        in_maps.append({
            "xT": np.ascontiguousarray(np.asarray(L0_attn_z[b], np.float32).T),
            "cT": np.ascontiguousarray(np.asarray(L0_c[b], np.float32).reshape(D, 1)),
            "wqkvT": wqkvT, "wmod": wmodc, "wgT": wgT, "wuT": wuT, "wdT": wdT,
            "ones128b": ones128b, "bdiagb": bd, "ones128f": ones128f,
            "onescolf": onescolf, "onecolbf": onecolbf, "identf": identf,
            "cstv": cstv,
        })
    res = run_bass_kernel_spmd(nc, in_maps, core_ids=list(range(B)))
    out = np.stack([np.ascontiguousarray(np.asarray(r["outT"]).T)
                    for r in res.results])
    return out.astype(np.float32)


# version 32 (final)
# speedup vs baseline: 1.0073x; 1.0073x over previous
import os as _os
import numpy as np
import concourse.bass as bass
import concourse.mybir as mybir
from concourse.bass_utils import run_bass_kernel_spmd

F32 = mybir.dt.float32
BF16 = mybir.dt.bfloat16
AF = mybir.ActivationFunctionType
ALU = mybir.AluOpType
AXX = mybir.AxisListType.X

D = 1024
T = 2048
NH = 16
DH = 64
NI = 4096
NDC = 8
NTC = 16
M64C = 1.5 * 2.0**52
M32C = 1.5 * 2.0**23
EPS = 1e-5


def mkap(t_ap, offset_elems, dims):
    return bass.AP(t_ap.tensor, offset_elems, dims)


def r3_in(src2d):
    return src2d.rearrange("(c p) e -> p c e", p=128)


def r3_out(dst2d, c):
    return dst2d.rearrange("p (c e) -> p c e", c=c)


class Prog:
    def __init__(self, nc):
        self.nc = nc
        self.guards = []
        self.sem = {}
        self.cnt = {}
        self.lastwait = {}
        for name in ["eT", "eV", "eS", "eG"]:
            g = nc.semaphore(name)
            self.sem[name] = g.__enter__()
            self.guards.append(g)
            self.cnt[name] = 0

    def dsem(self, name):
        if name not in self.sem:
            g = self.nc.semaphore(name)
            self.sem[name] = g.__enter__()
            self.guards.append(g)
            self.cnt[name] = 0
        return name

    def inc(self, ins, s, n=1):
        ins.then_inc(self.sem[s], n)
        self.cnt[s] += n
        return self.cnt[s]

    def dma(self, eng, engname, stream, out, in_, **kw):
        s = self.dsem(stream)
        lw = self.lastwait.get((engname, s), 0)
        if (self.cnt[s] - lw) >= 8 * 16:
            self.wait(eng, engname, s, self.cnt[s])
        self.inc(eng.dma_start(out=out, in_=in_, **kw), s, 16)
        return self.cnt[s]

    def wait(self, eng, engname, s, val=None):
        if val is None:
            val = self.cnt[s]
        if val <= 0:
            return
        key = (engname, s)
        if self.lastwait.get(key, 0) < val:
            eng.wait_ge(self.sem[s], val)
            self.lastwait[key] = val


class _Cut(Exception):
    pass


def build_nc():
    try:
        return _build_nc_inner()
    except _Cut as e:
        nc, fin = e.args
        fin()
        return nc


def _build_nc_inner():
    KPHASE = int(_os.environ.get("KPHASE", "99"))
    nc = bass.Bass("TRN2", target_bir_lowering=False)
    TE, V, S, G, Y = nc.tensor, nc.vector, nc.scalar, nc.gpsimd, nc.sync

    xT = nc.dram_tensor("xT", [D, T], F32, kind="ExternalInput").ap()
    cT = nc.dram_tensor("cT", [D, 1], F32, kind="ExternalInput").ap()
    wqkvT = nc.dram_tensor("wqkvT", [D, 3 * D], F32, kind="ExternalInput").ap()
    wmod = nc.dram_tensor("wmod", [D, 6 * D], F32, kind="ExternalInput").ap()
    wgT = nc.dram_tensor("wgT", [D, NI], F32, kind="ExternalInput").ap()
    wuT = nc.dram_tensor("wuT", [D, NI], F32, kind="ExternalInput").ap()
    wdT = nc.dram_tensor("wdT", [NI, D], F32, kind="ExternalInput").ap()
    ones128b = nc.dram_tensor("ones128b", [128, 128], BF16, kind="ExternalInput").ap()
    bdiagb = nc.dram_tensor("bdiagb", [128, 128], BF16, kind="ExternalInput").ap()
    ones128f = nc.dram_tensor("ones128f", [128, 128], F32, kind="ExternalInput").ap()
    onescolf = nc.dram_tensor("onescolf", [1, 128], F32, kind="ExternalInput").ap()
    onecolbf = nc.dram_tensor("onecolbf", [128, 1], BF16, kind="ExternalInput").ap()
    identf = nc.dram_tensor("identf", [128, 128], F32, kind="ExternalInput").ap()
    cstv_d = nc.dram_tensor("cstv", [128, 4], F32, kind="ExternalInput").ap()
    x2tmp = nc.dram_tensor("x2tmp", [D, T], F32, kind="Internal").ap()
    h2sp = nc.dram_tensor("h2sp", [D, T], F32, kind="Internal").ap()
    terng = nc.dram_tensor("terng", [128, 256 * 128], BF16, kind="Internal").ap()
    ternu = nc.dram_tensor("ternu", [128, 256 * 128], BF16, kind="Internal").ap()
    ternd = nc.dram_tensor("ternd", [128, 256 * 128], BF16, kind="Internal").ap()
    C8h = nc.dram_tensor("C8h", [NH, T], BF16, kind="Internal").ap()
    outT = nc.dram_tensor("outT", [D, T], F32, kind="ExternalOutput").ap()

    p = Prog(nc)
    eT, eV, eS, eG = "eT", "eV", "eS", "eG"

    def alloc(name, shape, dtype, side="left"):
        g = nc.sbuf_tensor(name, shape, dtype, side=side)
        return g, g.__enter__()[:]

    blkg = nc.Block()
    blkg.__enter__()

    def _finish(tile, n, m):
        p.wait(Y, "Y", eV)
        p.wait(Y, "Y", eS)
        p.wait(Y, "Y", eT)
        p.wait(Y, "Y", eG)
        p.dma(G, "G", "dcut", outT[0:n, 0:m], tile)
        p.wait(G, "G", "dcut")
        blkg.__exit__(None, None, None)

    pg = []

    def palloc(name, shape, dtype, side="left"):
        g, t = alloc(name, shape, dtype, side)
        pg.append(g)
        return t

    onesb = palloc("onesb", [128, 128], BF16)
    bdiag = palloc("bdiag", [128, 128], BF16)
    onesf = palloc("onesf", [128, 128], F32)
    onecf = palloc("onecf", [1, 128], F32)
    onecb = palloc("onecb", [128, 1], BF16)
    ident = palloc("ident", [128, 128], F32)
    g1m = palloc("g1m", [128, 1], F32)
    modv = palloc("modv", [128, 48], F32)
    avec = palloc("avec", [128, 8], F32)
    bvec = palloc("bvec", [128, 8], F32)
    gav = palloc("gav", [128, 8], F32)
    gbv = palloc("gbv", [128, 8], F32)
    negC = palloc("negC", [128, 16], F32)
    Nq = palloc("Nq", [128, 8], F32)
    Nk = palloc("Nk", [128, 8], F32)
    mstat = palloc("mstat", [128, 8], F32)
    Max1c = palloc("Max1c", [128, 16], F32)
    Max2c = palloc("Max2c", [128, 16], F32)
    Mtok = palloc("Mtok", [128, 16], F32)
    smallw = palloc("smallw", [128, 16], F32)
    S16 = palloc("S16", [16, 128], F32)
    cstv = palloc("cstv_s", [128, 4], F32)
    hA = palloc("hA", [128, NDC * T], BF16)

    for dst, src in [(onesb, ones128b), (bdiag, bdiagb), (onesf, ones128f),
                     (onecf, onescolf), (onecb, onecolbf), (ident, identf),
                     (cstv, cstv_d)]:
        p.dma(Y, "Y", "dc", dst, src)

    p.wait(V, "V", eV)
    p.inc(V.memset(g1m, 1.0), eV)
    p.wait(V, "V", eV)
    p.wait(V, "V", eV)
    p.inc(V.tensor_scalar(g1m, g1m, 2.0**-30, 1.0, ALU.add, ALU.is_le), eV)

    g_modrow, modrow = alloc("modrow", [1, 6 * D], F32, "right")
    g_wmb0, wmb0 = alloc("wmb0", [128, 4096], F32, "right")
    g_wmb1, wmb1 = alloc("wmb1", [128, 4096], F32, "right")
    g_ccol, ccol = alloc("ccol", [128, 8], F32, "right")
    g_silc, silc = alloc("silc", [128, 8], F32, "right")
    wmb = [wmb0, wmb1]

    p.dma(Y, "Y", "dc", ccol, cT.rearrange("(c p) o -> p (c o)", p=128),
          allow_slow_non_contiguous=True)
    tok_cc = p.cnt["dc"]
    p.wait(S, "S", "dc", tok_cc)
    p.wait(S, "S", eS)
    p.inc(S.activation(silc, ccol, AF.Silu), eS)
    tok_silc = p.cnt[eS]

    with nc.psum_tensor("modps", [1, 4096], F32) as mpt:
        modps = mpt[:]
        mm_tok = {}
        for half, (c0, cw) in enumerate([(0, 4096), (4096, 2048)]):
            for j in range(NDC):
                buf = j % 2
                kk0 = half * NDC + j
                if kk0 >= 2:
                    p.wait(Y, "Y", eT, mm_tok[kk0 - 2])
                dm = "dm%d" % buf
                tok_w = p.dma(Y, "Y", dm, wmb[buf][:, 0:cw],
                              wmod[j * 128:(j + 1) * 128, c0:c0 + cw])
                p.wait(TE, "T", "dc")
                p.wait(TE, "T", dm, tok_w)
                p.wait(TE, "T", eS, tok_silc)
                if half == 1 and j == 0:
                    p.wait(TE, "T", eV)
                for grp in range(cw // 512):
                    p.inc(TE.matmul(modps[0:1, grp * 512:(grp + 1) * 512],
                                    silc[:, j:j + 1], wmb[buf][:, grp * 512:(grp + 1) * 512],
                                    start=(j == 0), stop=(j == NDC - 1)), eT)
                mm_tok[half * NDC + j] = p.cnt[eT]
            p.wait(V, "V", eT)
            p.wait(V, "V", eV)
            p.inc(V.tensor_copy(modrow[0:1, c0:c0 + cw], modps[0:1, 0:cw]), eV)
    tok_modrow = p.cnt[eV]

    with nc.psum_tensor("mvps", [128, 48], F32) as mvt:
        mvps = mvt[:]
        p.wait(TE, "T", eV, tok_modrow)
        for k48 in range(48):
            p.inc(TE.transpose(mvps[:, k48:k48 + 1],
                               modrow[0:1, k48 * 128:(k48 + 1) * 128],
                               ident[0:1, 0:1]), eT)
        tok_mv = p.cnt[eT]
        p.wait(V, "V", eT, tok_mv)
        p.wait(V, "V", eV)
        p.inc(V.tensor_copy(modv, mvps), eV)
    tok_modv = p.cnt[eV]
    tok_mv_T = tok_mv

    p.wait(V, "V", eV, tok_modv)
    p.wait(V, "V", eV)
    p.inc(V.tensor_scalar(avec, modv[:, 8:16], 1.0, None, ALU.add), eV)
    p.wait(V, "V", eV)
    p.inc(V.tensor_scalar(bvec, modv[:, 32:40], 1.0, None, ALU.add), eV)
    for (dstv, col) in [(gav, 16), (gbv, 40)]:
        ga = modv[:, col:col + 8]
        tq = smallw[:, 0:8]
        uq = smallw[:, 8:16]
        p.wait(V, "V", eV)
        p.wait(V, "V", eV)
        p.inc(V.tensor_scalar(tq, ga, 0.25, None, ALU.max), eV)
        p.wait(V, "V", eV)
        p.inc(V.tensor_scalar(uq, ga, -0.25, None, ALU.min), eV)
        p.wait(V, "V", eV)
        p.inc(V.tensor_scalar(dstv, ga, 0.0, None, ALU.is_ge), eV)
        p.wait(V, "V", eV)
        p.wait(V, "V", eV)
        p.inc(V.tensor_sub(tq, tq, uq), eV)
        p.wait(V, "V", eV)
        p.wait(V, "V", eV)
        p.inc(V.tensor_mul(dstv, dstv, tq), eV)
        p.wait(V, "V", eV)
        p.wait(V, "V", eV)
        p.inc(V.tensor_add(dstv, dstv, uq), eV)
    tok_modvecs = p.cnt[eV]
    if KPHASE < 2:
        _finish(modv, 128, 48)
        return nc

    for g in [g_silc, g_ccol, g_wmb1, g_wmb0, g_modrow]:
        g.__exit__(None, None, None)

    g_xs0, xs0 = alloc("xs0", [128, T], F32, "right")
    g_xs1, xs1 = alloc("xs1", [128, T], F32, "right")
    g_xb0, xb0 = alloc("xb0", [128, T], BF16, "right")
    g_xb1, xb1 = alloc("xb1", [128, T], BF16, "right")
    g_xq0, xq0 = alloc("xq0", [128, T], BF16, "right")
    g_xq1, xq1 = alloc("xq1", [128, T], BF16, "right")
    g_Rb, Rb = alloc("Rb", [128, T], F32, "right")
    g_MRb, MRb = alloc("MRb", [128, T], F32, "right")
    g_lsc, lsc = alloc("lsc", [128, 3 * 512], F32, "right")
    g_tscr, tscr0 = alloc("tscr", [128, T], F32, "right")
    g_tscr1, tscr1 = alloc("tscr1", [128, T], F32, "right")
    tscr = [tscr0, tscr1]
    xs, xbt, xqt = [xs0, xs1], [xb0, xb1], [xq0, xq1]

    def emit_ln_stats(src_dram, S1ps, S2ps, first_wait=None):
        use_tok = {}
        for d in range(NDC):
            buf = d % 2
            if d == 0 and first_wait is not None:
                first_wait()
            if d >= 2:
                p.wait(Y, "Y", eV, use_tok[(d - 2, "v")])
                p.wait(Y, "Y", eS, use_tok[(d - 2, "s")])
            dxs = "dx%d" % buf
            tok_x = p.dma(Y, "Y", dxs, xs[buf],
                          src_dram[d * 128:(d + 1) * 128, :])
            p.wait(V, "V", dxs, tok_x)
            if d >= 2:
                p.wait(V, "V", eT, use_tok[(d - 2, "mm")])
                p.wait(S, "S", eT, use_tok[(d - 2, "mm")])
            p.wait(V, "V", eV)
            p.inc(V.tensor_copy(xbt[buf], xs[buf]), eV)
            use_tok[(d, "v")] = p.cnt[eV]
            p.wait(S, "S", dxs, tok_x)
            p.wait(S, "S", eS)
            p.inc(S.activation(xqt[buf], xs[buf], AF.Square), eS)
            use_tok[(d, "s")] = p.cnt[eS]
            p.wait(TE, "T", eV, use_tok[(d, "v")])
            p.wait(TE, "T", eS, use_tok[(d, "s")])
            for gq in range(4):
                sl = slice(gq * 512, (gq + 1) * 512)
                p.inc(TE.matmul(S1ps[:, sl], onesb, xbt[buf][:, sl],
                                start=(d == 0), stop=(d == NDC - 1)), eT)
                p.inc(TE.matmul(S2ps[:, sl], onesb, xqt[buf][:, sl],
                                start=(d == 0), stop=(d == NDC - 1)), eT)
            use_tok[(d, "mm")] = p.cnt[eT]
        return p.cnt[eT]

    def emit_ln_rb(S1ps, S2ps, Rbo, MRbo, tok_stats):
        mub = lsc[:, 0:512]
        vrb = lsc[:, 512:1024]
        tmb = lsc[:, 1024:1536]
        for gq in range(4):
            sl = slice(gq * 512, (gq + 1) * 512)
            p.wait(V, "V", eT, tok_stats)
            if gq > 0:
                p.wait(V, "V", eS)
                p.wait(V, "V", eV)
            p.wait(V, "V", eV)
            p.inc(V.tensor_scalar(mub, S1ps[:, sl], 1.0 / D, None, ALU.mult), eV)
            tok_mu = p.cnt[eV]
            p.wait(V, "V", eV)
            p.inc(V.tensor_scalar(vrb, S2ps[:, sl], 1.0 / D, None, ALU.mult), eV)
            p.wait(S, "S", eV, tok_mu)
            p.wait(S, "S", eS)
            p.inc(S.activation(MRbo[:, sl], mub, AF.Square), eS)
            tok_mq = p.cnt[eS]
            p.wait(V, "V", eS, tok_mq)
            p.wait(V, "V", eV)
            p.inc(V.tensor_sub(vrb, vrb, MRbo[:, sl]), eV)
            tvv = p.cnt[eV]
            p.wait(S, "S", eV, tvv)
            p.wait(S, "S", eS)
            p.inc(S.activation(tmb, vrb, AF.Sqrt, bias=cstv[:, 0:1]), eS)
            p.wait(V, "V", eS)
            p.wait(V, "V", eV)
            p.inc(V.reciprocal(Rbo[:, sl], tmb), eV)
            p.wait(V, "V", eV)
            p.wait(V, "V", eV)
            p.inc(V.tensor_mul(MRbo[:, sl], mub, Rbo[:, sl]), eV)
        return p.cnt[eV]

    def emit_ln_apply(src_dram, scale_vec, shift_vec, dest_tile, Rbo, MRbo,
                      tok_rb, dtoks=None):
        use_tok = {}
        stok = {}
        for d in range(NDC):
            buf = d % 2
            if d < 2:
                p.wait(Y, "Y", eV)
                p.wait(Y, "Y", eS)
            if d >= 2:
                p.wait(Y, "Y", eV, use_tok[d - 2])
            dxs = "dx%d" % buf
            tok_x = p.dma(Y, "Y", dxs, xs[buf],
                          src_dram[d * 128:(d + 1) * 128, :])
            p.wait(V, "V", dxs, tok_x)
            p.wait(V, "V", eV, tok_rb)
            if d >= 2:
                p.wait(V, "V", eS, stok[d - 2])
            p.wait(V, "V", eV)
            p.inc(V.tensor_mul(tscr[buf], xs[buf], Rbo), eV)
            use_tok[d] = p.cnt[eV]
            p.wait(V, "V", eV)
            p.inc(V.tensor_sub(tscr[buf], tscr[buf], MRbo), eV)
            tok_n = p.cnt[eV]
            p.wait(S, "S", eV, tok_n)
            p.wait(S, "S", eS)
            p.inc(S.activation(dest_tile(d), tscr[buf], AF.Identity,
                               scale=scale_vec[:, d:d + 1],
                               bias=shift_vec[:, d:d + 1]), eS)
            stok[d] = p.cnt[eS]
            if dtoks is not None:
                dtoks[d] = p.cnt[eS]
        return p.cnt[eS]

    with (nc.psum_tensor("S1psA", [128, T], F32) as s1t,
          nc.psum_tensor("S2psA", [128, T], F32) as s2t):
        def _fw():
            p.wait(Y, "Y", eT, tok_mv_T)
            p.wait(Y, "Y", eV, tok_modrow)
        tok_stats = emit_ln_stats(xT, s1t[:], s2t[:], _fw)
        tok_rb = emit_ln_rb(s1t[:], s2t[:], Rb, MRb, tok_stats)
    p.wait(V, "V", eV, tok_modvecs)
    lnap_tok = {}
    tok_h = emit_ln_apply(xT, avec, modv[:, 0:8],
                          lambda d: hA[:, d * T:(d + 1) * T], Rb, MRb, tok_rb,
                          dtoks=lnap_tok)
    if KPHASE < 3:
        _finish(hA[:, 0:2048], 128, 2048)
        return nc

    for g in [g_tscr1, g_tscr, g_lsc, g_MRb, g_Rb, g_xq1, g_xq0, g_xb1, g_xb0,
              g_xs1, g_xs0]:
        g.__exit__(None, None, None)

    g_Qc, Qc = alloc("Qc", [128, NDC * T], BF16, "left")
    g_Kc, Kc = alloc("Kc", [128, NDC * T], BF16, "left")
    g_Vc, Vc = alloc("Vc", [128, NTC * 1040], BF16, "left")

    g_wc0, wc0 = alloc("wc0", [128, 1024], BF16, "left")
    g_wc1, wc1 = alloc("wc1", [128, 1024], BF16, "left")
    wcol = [wc0, wc1]

    ones_ap = mkap(Vc, 64, [[NTC * 1040, 128], [1040, NTC], [130, 8], [1, 2]])
    p.wait(V, "V", eV)
    p.inc(V.memset(ones_ap, 1.0), eV)
    tok_vones = p.cnt[eV]

    g_mb0, mb0 = alloc("mb0", [128, 4096], F32, "right")
    g_mb1, mb1 = alloc("mb1", [128, 4096], F32, "right")
    g_macc, macc = alloc("macc", [128, 24], F32, "right")
    mbuf = [mb0, mb1]
    mean_plan = []
    for (wsrc, base) in [(wgT, 0), (wuT, 8), (wdT, 16)]:
        for j in range(8):
            mean_plan.append((wsrc, base, j))
    mean_state = {"k": 0, "red": {}}

    def emit_mean_unit():
        k = mean_state["k"]
        if k >= len(mean_plan):
            return
        wsrc, base, j = mean_plan[k]
        red_tok = mean_state["red"]
        if k == 0:
            p.wait(Y, "Y", eS, tok_h)
            p.wait(Y, "Y", eV)
            p.wait(Y, "Y", eT)
        if wsrc.shape[0] == D:
            src = wsrc[j * 128:(j + 1) * 128, :]
        else:
            src = r3_in(wsrc[j * 512:(j + 1) * 512, :])
        buf = k % 2
        if k >= 2:
            sem_e, tok_e = red_tok[k - 2]
            p.wait(Y, "Y", sem_e, tok_e)
        outm = mbuf[buf] if wsrc.shape[0] == D else r3_out(mbuf[buf], 4)
        dns = "dn%d" % buf
        tok_m = p.dma(Y, "Y", dns, outm, src)
        dstc = macc[:, base + j:base + j + 1]
        p.wait(V, "V", dns, tok_m)
        p.wait(V, "V", eV)
        p.inc(V.reduce_sum(dstc, mbuf[buf], axis=AXX,
                           apply_absolute_value=True), eV)
        red_tok[k] = (eV, p.cnt[eV])
        mean_state["k"] = k + 1

    with (nc.psum_tensor("qkpsA", [128, T], F32) as qa,
          nc.psum_tensor("qkpsB", [128, T], F32) as qb):
        qkps = [qa[:], qb[:]]
        evac_tok = {0: (eV, 0), 1: (eV, 0)}
        mm_tok = {}
        for ec in range(16):
            buf = ec % 2
            if ec >= 2:
                p.wait(G, "G", eT, mm_tok[ec - 2])
            src = r3_in(wqkvT[:, ec * 128:(ec + 1) * 128])
            dqs = "dq%d" % buf
            tok_w = p.dma(G, "G", dqs, r3_out(wcol[buf], 8), src)
            p.wait(TE, "T", dqs, tok_w)
            sem_e, tok_e = evac_tok[buf]
            p.wait(TE, "T", sem_e, tok_e)
            for d in range(NDC):
                if ec == 0:
                    p.wait(TE, "T", eS, lnap_tok[d])
                for gq in range(4):
                    sl = slice(gq * 512, (gq + 1) * 512)
                    p.inc(TE.matmul(qkps[buf][:, sl],
                                    wcol[buf][:, d * 128:(d + 1) * 128],
                                    hA[:, d * T + gq * 512: d * T + (gq + 1) * 512],
                                    start=(d == 0), stop=(d == NDC - 1)), eT)
            mm_tok[ec] = p.cnt[eT]
            dst = (Qc if ec < 8 else Kc)[:, (ec % 8) * T:((ec % 8) + 1) * T]
            if ec % 2 == 0:
                p.wait(V, "V", eT, mm_tok[ec])
                p.wait(V, "V", eV)
                p.inc(V.tensor_copy(dst, qkps[buf]), eV)
                evac_tok[buf] = (eV, p.cnt[eV])
            else:
                p.wait(S, "S", eT, mm_tok[ec])
                p.wait(S, "S", eS)
                p.inc(S.activation(dst, qkps[buf], AF.Copy), eS)
                evac_tok[buf] = (eS, p.cnt[eS])
            emit_mean_unit()
        tok_qk_v = p.cnt[eV]
        tok_qk_s = p.cnt[eS]

    g_wv, wvt = alloc("wvt", [128, NDC * 1024], BF16, "right")
    src = r3_in(wqkvT[:, 2048:3072])
    tok_wv = p.dma(G, "G", "dv", r3_out(wvt, 8), src)
    with (nc.psum_tensor("vpsA", [128, 1024], F32) as va,
          nc.psum_tensor("vpsB", [128, 1024], F32) as vb):
        vps = [va[:], vb[:]]
        vevac = [0, 0]
        for t in range(NTC):
            buf = t % 2
            if t == 0:
                p.wait(TE, "T", eV, tok_qk_v)
                p.wait(TE, "T", eS, tok_qk_s)
            p.wait(TE, "T", "dv", tok_wv)
            p.wait(TE, "T", eV, vevac[buf])
            for d in range(NDC):
                lhsT = hA[:, d * T + t * 128: d * T + (t + 1) * 128]
                for n2 in range(2):
                    p.inc(TE.matmul(vps[buf][:, n2 * 512:(n2 + 1) * 512], lhsT,
                                    wvt[:, d * 1024 + n2 * 512: d * 1024 + (n2 + 1) * 512],
                                    start=(d == 0), stop=(d == NDC - 1)), eT)
            tok_vmm = p.cnt[eT]
            dst = mkap(Vc, t * 1040, [[NTC * 1040, 128], [130, 8], [66, 2], [1, 64]])
            p.wait(V, "V", eT, tok_vmm)
            p.wait(V, "V", eV, tok_vones)
            p.wait(V, "V", eV)
            p.inc(V.tensor_copy(dst, vps[buf]), eV)
            vevac[buf] = p.cnt[eV]
            if t % 2 == 0:
                emit_mean_unit()
    tok_v = p.cnt[eV]
    g_wv.__exit__(None, None, None)
    while mean_state["k"] < len(mean_plan):
        emit_mean_unit()

    p.wait(V, "V", eG)
    p.wait(V, "V", eV)
    with nc.psum_tensor("mredps", [128, 4], F32) as mrt:
        mred = mrt[:]
        for mi in range(3):
            p.wait(V, "V", eV)
            p.inc(V.reduce_sum(smallw[:, mi:mi + 1], macc[:, mi * 8:mi * 8 + 8],
                               axis=AXX), eV)
        tok_colsum = p.cnt[eV]
        p.wait(TE, "T", eV, tok_colsum)
        for mi in range(3):
            p.inc(TE.matmul(mred[:, mi:mi + 1], onesf, smallw[:, mi:mi + 1],
                            start=True, stop=True), eT)
        tok_mred = p.cnt[eT]
        p.wait(V, "V", eT, tok_mred)
        for mi in range(3):
            p.wait(V, "V", eV)
            p.inc(V.tensor_scalar(mstat[:, 2 * mi:2 * mi + 1], mred[:, mi:mi + 1],
                                  1.0 / (NI * D), 1e-5, ALU.mult, ALU.max), eV)
        p.wait(V, "V", eV)
        for mi in range(3):
            p.wait(V, "V", eV)
            p.inc(V.reciprocal(mstat[:, 2 * mi + 1:2 * mi + 2],
                               mstat[:, 2 * mi:2 * mi + 1]), eV)
    tok_mstat = p.cnt[eV]
    for g in [g_macc, g_mb1, g_mb0]:
        g.__exit__(None, None, None)

    g_sqb, sqb = alloc("sqb", [128, T], BF16, "right")
    g_sqq, sqq = alloc("sqq", [128, T], BF16, "right")
    g_qnr, qnr = alloc("qnr", [128, T], F32, "right")
    g_c8s, c8s = alloc("c8s", [128, T], BF16, "right")
    with (nc.psum_tensor("nps", [128, T], F32) as npt,
          nc.psum_tensor("npsq", [128, T], F32) as npqt):
        nps = npt[:]
        npsq = npqt[:]
        kmax_tok = {}
        kmm_tok = {}
        qmm_tok = {}
        negk_tok = {}
        c8v_tok = {}
        c8d_tok = {}

        def emit_knorm(c):
            p.wait(S, "S", eV, tok_mstat)
            p.wait(S, "S", eV, tok_qk_v)
            p.wait(S, "S", eS, tok_qk_s)
            if c >= 1:
                p.wait(S, "S", eT, kmm_tok[c - 1])
            p.wait(S, "S", eS)
            p.inc(S.activation(sqb, Kc[:, c * T:(c + 1) * T], AF.Square), eS)
            tok_sq = p.cnt[eS]
            p.wait(TE, "T", eS, tok_sq)
            if c >= 1:
                p.wait(TE, "T", eV, kmax_tok[c - 1])
            for gq in range(4):
                sl = slice(gq * 512, (gq + 1) * 512)
                p.inc(TE.matmul(nps[:, sl], bdiag, sqb[:, sl],
                                start=True, stop=True), eT)
            kmm_tok[c] = p.cnt[eT]
            p.wait(V, "V", eT, kmm_tok[c])
            p.wait(V, "V", eV)
            p.inc(V.reduce_max(Nk[:, c:c + 1], nps, axis=AXX), eV)
            kmax_tok[c] = p.cnt[eV]
            p.wait(S, "S", eV, kmax_tok[c])
            p.wait(S, "S", eS)
            p.inc(S.activation(Nq[:, c:c + 1], Nk[:, c:c + 1], AF.Sqrt), eS)
            tok_sn = p.cnt[eS]
            p.wait(V, "V", eS, tok_sn)
            p.wait(V, "V", eV)
            p.inc(V.tensor_scalar(Nq[:, c:c + 1], Nq[:, c:c + 1], -1.0,
                                  None, ALU.mult), eV)
            negk_tok[c] = p.cnt[eV]

        def emit_qnorm(c):
            p.wait(S, "S", eV, negk_tok[c])
            if c >= 1:
                p.wait(S, "S", eT, qmm_tok[c - 1])
            p.wait(S, "S", eS)
            p.inc(S.activation(sqq, Qc[:, c * T:(c + 1) * T], AF.Square), eS)
            tok_sq = p.cnt[eS]
            p.wait(TE, "T", eS, tok_sq)
            if c >= 1:
                p.wait(TE, "T", eS)
            for gq in range(4):
                sl = slice(gq * 512, (gq + 1) * 512)
                p.inc(TE.matmul(npsq[:, sl], bdiag, sqq[:, sl],
                                start=True, stop=True), eT)
            qmm_tok[c] = p.cnt[eT]
            p.wait(S, "S", eT, qmm_tok[c])
            if c >= 1:
                p.wait(S, "S", eV, c8v_tok[c - 1])
            p.wait(S, "S", eS)
            p.inc(S.activation(qnr, npsq, AF.Sqrt), eS)
            tok_qn = p.cnt[eS]
            p.wait(V, "V", eS, tok_qn)
            if c >= 1:
                p.wait(V, "V", "dc8", c8d_tok[c - 1])
            p.wait(V, "V", eV)
            p.inc(V.tensor_scalar(c8s, qnr, Nq[:, c:c + 1], None, ALU.mult), eV)
            c8v_tok[c] = p.cnt[eV]
            for par in range(2):
                hh = 2 * c + par
                rsl = slice(64 * par, 64 * par + 1)
                p.wait(Y, "Y", eV, c8v_tok[c])
                p.dma(Y, "Y", "dc8", C8h[hh:hh + 1, :], c8s[rsl, :])
            c8d_tok[c] = p.cnt["dc8"]

        for c in range(NDC):
            emit_knorm(c)
            if c >= 1:
                emit_qnorm(c - 1)
        emit_qnorm(NDC - 1)
    tok_c8 = p.cnt["dc8"]
    tok_negC = p.cnt[eV]
    tok_negC_T = p.cnt[eT]
    tok_p2_s = p.cnt[eS]
    if KPHASE < 4:
        _finish(Qc[:, 0:2048], 128, 2048)
        return nc
    for g in [g_c8s, g_qnr, g_sqq, g_sqb, g_wc1, g_wc0]:
        g.__exit__(None, None, None)

    g_drow, drow = alloc("drow", [1, 512], F32, "right")
    g_bcsb, bcsb = alloc("bcsb", [128, 512], BF16, "right")
    g_c80, c80 = alloc("c80", [1, 1024], BF16, "right")
    g_tw0, tw0 = alloc("tw0", [128, 256], F32, "right")
    g_tw1, tw1 = alloc("tw1", [128, 256], F32, "right")
    g_tw2, tw2 = alloc("tw2", [128, 256], F32, "right")
    g_tw3, tw3 = alloc("tw3", [128, 256], F32, "right")
    g_tta, tta = alloc("tta", [128, 256], BF16, "right")
    g_ttb, ttb = alloc("ttb", [128, 256], BF16, "right")
    g_tto0, tto0 = alloc("tto0", [128, 256], BF16, "right")
    g_tto1, tto1 = alloc("tto1", [128, 256], BF16, "right")
    g_tto2, tto2 = alloc("tto2", [128, 256], BF16, "right")
    g_tto3, tto3 = alloc("tto3", [128, 256], BF16, "right")
    twb = [tw0, tw1, tw2, tw3]
    ttо = [tto0, tto1, tto2, tto3]

    PW = 256
    tern_plan = []
    for (wsrc_, tdst_, rmc_, ssem_, nb_) in [
            (wgT, terng, mstat[:, 1:2], "dwg", 8),
            (wuT, ternu, mstat[:, 3:4], "dwu", 8),
            (wdT, ternd, mstat[:, 5:6], "dwd", 32)]:
        for r_ in range(wsrc_.shape[0] // 128):
            for cc_ in range(wsrc_.shape[1] // PW):
                base_ = (2 * cc_ * nb_ + r_) * 128
                tern_plan.append((wsrc_, tdst_, rmc_, ssem_, r_, cc_, nb_, base_))
    tern_state = {"kk": 0, "gtok": {}, "stok": {}}

    def emit_tern_piece():
        kk = tern_state["kk"]
        if kk >= len(tern_plan):
            return
        gtok = tern_state["gtok"]
        wsrc_, tdst_, rmc_, ssem_, r_, cc_, nb_, base_ = tern_plan[kk]
        buf = kk % 4
        dts = "dt%d" % buf
        if kk < 4:
            p.wait(G, "G", eV, tok_mstat)
            p.wait(G, "G", eV, tok_negC)
            p.wait(G, "G", eT, tok_negC_T)
            p.wait(G, "G", eS, tok_p2_s)
        sname = ssem_ + str(buf)
        if kk >= 4:
            p.wait(G, "G", eV, gtok[kk - 4])
        tok_w = p.dma(G, "G", dts, twb[buf],
                      wsrc_[r_ * 128:(r_ + 1) * 128, cc_ * PW:(cc_ + 1) * PW])
        p.wait(V, "V", dts, tok_w)
        p.wait(V, "V", "dc8", tok_c8)
        p.wait(V, "V", eV, tok_mstat)
        p.wait(V, "V", eV)
        p.inc(V.tensor_scalar(tta, twb[buf], rmc_, 0.5, ALU.mult, ALU.is_gt), eV)
        p.inc(V.tensor_scalar(ttb, twb[buf], rmc_, -0.5, ALU.mult, ALU.is_lt), eV)
        if kk >= 4:
            ps, pt = tern_state["stok"][kk - 4]
            p.wait(V, "V", ps, pt)
        p.wait(V, "V", eV)
        p.inc(V.tensor_sub(ttо[buf], tta, ttb), eV)
        gtok[kk] = p.cnt[eV]
        p.wait(Y, "Y", eV, gtok[kk])
        dst3 = mkap(tdst_, base_, [[256 * 128, 128], [nb_ * 128, 2], [1, 128]])
        p.dma(Y, "Y", sname, dst3,
              ttо[buf].rearrange("p (j e) -> p j e", j=2))
        tern_state["stok"][kk] = (sname, p.cnt[sname])
        tern_state["kk"] = kk + 1

    def emit_tern_n(n):
        for _ in range(n):
            emit_tern_piece()


    g_PT0, PT0 = alloc("PT0", [128, NTC * 1024], BF16, "left")
    g_PT1, PT1 = alloc("PT1", [128, NTC * 1024], BF16, "left")
    PT = [PT0, PT1]

    with (nc.psum_tensor("sps0", [128, 1024], F32) as s0t,
          nc.psum_tensor("sps1", [128, 1024], F32) as s1t_,
          nc.psum_tensor("avps0", [128, 512], F32) as av0t,
          nc.psum_tensor("avps1", [128, 512], F32) as av1t,
          nc.psum_tensor("dops", [1, 512], F32) as dot,
          nc.psum_tensor("bcps", [128, 512], F32) as bct):
        sps = [s0t[:], s1t_[:]]
        avps = [av0t[:], av1t[:]]
        dops = dot[:]
        bcps = bct[:]
        exp_sbuf_tok = [0, 0]
        pt_avdone = [0, 0]
        av_evac_tok = [0, 0]
        do_read_tok = 0
        bc_read_tok = 0
        drow_read = 0

        p.wait(TE, "T", eV, tok_v)
        p.wait(TE, "T", eV, tok_qk_v)
        p.wait(TE, "T", eV, tok_negC)
        p.wait(TE, "T", eS, tok_qk_s)
        c8tok = {}
        p.wait(Y, "Y", "dc8", tok_c8)
        c8row_tok = {(0, 0): p.dma(Y, "Y", "dc80", c80, C8h[0:1, 0:1024])}
        for h in range(NH):
            c = h // 2
            par = h % 2
            bp = 64 * par
            cb = c80
            for half in range(2):
                hb = half
                tok_c8row = c8row_tok[(h, half)]
                for kt in range(NTC):
                    sp = kt % 2
                    p.wait(TE, "T", eS, exp_sbuf_tok[sp])
                    if kt == 0:
                        p.wait(TE, "T", "dc80", tok_c8row)
                    lhsT = Kc[bp:bp + 64, c * T + kt * 128: c * T + (kt + 1) * 128]
                    for n2 in range(2):
                        cs = n2 * 512
                        p.inc(TE.matmul(sps[sp][:, n2 * 512:(n2 + 1) * 512],
                                        onesb[0:1, :], cb[0:1, cs:cs + 512],
                                        start=True, stop=False), eT)
                    if kt == NTC - 1:
                        tok_lastr1 = p.cnt[eT]
                    for n2 in range(2):
                        q0 = c * T + half * 1024 + n2 * 512
                        p.inc(TE.matmul(sps[sp][:, n2 * 512:(n2 + 1) * 512],
                                        lhsT, Qc[bp:bp + 64, q0:q0 + 512],
                                        start=False, stop=True,
                                        tile_position=(bp, 0)), eT)
                    tok_sc = p.cnt[eT]
                    p.wait(S, "S", eT, tok_sc)
                    if kt == 0:
                        p.wait(S, "S", eT, pt_avdone[hb])
                    p.wait(S, "S", eS)
                    p.inc(S.activation(PT[hb][:, kt * 1024:(kt + 1) * 1024],
                                       sps[sp], AF.Exp, scale=0.125), eS)
                    exp_sbuf_tok[sp] = p.cnt[eS]
                    tok_lastsc = p.cnt[eT]
                    if 2 <= kt <= 13:
                        emit_tern_piece()
                tok_exp_all = p.cnt[eS]
                nh, nhalf = (h, 1) if half == 0 else (h + 1, 0)
                if nh < NH:
                    p.wait(Y, "Y", eT, tok_lastr1)
                    c8row_tok[(nh, nhalf)] = p.dma(
                        Y, "Y", "dc80", c80,
                        C8h[nh:nh + 1, nhalf * 1024:(nhalf + 1) * 1024])
                for gg in range(2):
                    gq = half * 2 + gg
                    avb = gg
                    p.wait(TE, "T", eS, tok_exp_all)
                    p.wait(TE, "T", eV, av_evac_tok[avb])
                    if par == 1:
                        p.wait(TE, "T", eS, do_read_tok)
                    for kt in range(NTC):
                        rhs = PT[hb][:, kt * 1024 + gg * 512: kt * 1024 + (gg + 1) * 512]
                        base = kt * 1040 + c * 130
                        if par == 0:
                            p.inc(TE.matmul(avps[avb][0:65, :],
                                            Vc[:, base:base + 65], rhs,
                                            start=(kt == 0), stop=(kt == NTC - 1)), eT)
                        else:
                            p.inc(TE.matmul(avps[avb][64:128, :],
                                            Vc[:, base + 66:base + 130], rhs,
                                            start=(kt == 0), stop=(kt == NTC - 1),
                                            tile_position=(0, 64)), eT)
                            p.inc(TE.matmul(dops[0:1, :], onecb, rhs,
                                            start=(kt == 0), stop=(kt == NTC - 1)), eT)
                    tok_av = p.cnt[eT]
                    if gg == 1:
                        pt_avdone[hb] = tok_av
                    if half == 1 and gg == 1:
                        c8tok[h] = p.cnt[eT]
                    p.wait(S, "S", eT, tok_av)
                    p.wait(S, "S", eT, drow_read)
                    p.wait(S, "S", eS)
                    if par == 0:
                        p.inc(S.activation(drow, avps[avb][64:65, :], AF.Copy), eS)
                    else:
                        p.inc(S.activation(drow, dops[0:1, :], AF.Copy), eS)
                        do_read_tok = p.cnt[eS]
                    tok_dr = p.cnt[eS]
                    p.wait(TE, "T", eS, tok_dr)
                    p.wait(TE, "T", eV, bc_read_tok)
                    p.inc(TE.matmul(bcps[:], onecf, drow, start=True, stop=True), eT)
                    tok_bc = p.cnt[eT]
                    p.wait(V, "V", eT, tok_bc)
                    p.wait(V, "V", eV)
                    with nc.allow_low_precision(
                            reason="fp32 reciprocal, single RNE round to bf16;"
                                   " identical to prior fp32-recip+copy"):
                        p.inc(V.reciprocal(bcsb, bcps[:]), eV)
                    bc_read_tok = p.cnt[eV]
                    drow_read = tok_bc
                    dst = hA[bp:bp + 64, c * T + gq * 512: c * T + (gq + 1) * 512]
                    srcav = avps[avb][0:64, :] if par == 0 else avps[avb][64:128, :]
                    bcs = bcsb[0:64, :] if par == 0 else bcsb[64:128, :]
                    p.wait(V, "V", eV)
                    p.inc(V.tensor_tensor(dst, srcav, bcs, ALU.mult), eV)
                    av_evac_tok[avb] = p.cnt[eV]
        tok_attn = p.cnt[eV]
        tok_p3T = p.cnt[eT]

    emit_tern_n(len(tern_plan))
    tern_streams = [m + str(b) for m in ("dwg", "dwu", "dwd") for b in range(4)
                    if (m + str(b)) in p.cnt]
    tok_terns = {s_: p.cnt[s_] for s_ in tern_streams}
    for g in [g_tto3, g_tto2, g_tto1, g_tto0, g_ttb, g_tta,
              g_tw3, g_tw2, g_tw1, g_tw0,
              g_c80, g_bcsb, g_drow, g_PT1, g_PT0, g_Vc, g_Kc, g_Qc]:
        g.__exit__(None, None, None)
    if KPHASE < 7:
        _finish(hA[:, 0:2048], 128, 2048)
        return nc

    g_alb, alphab = alloc("alphab", [128, T], F32, "right")
    g_mabs, mabs = alloc("mabs", [128, T], F32, "right")
    g_S16b, S16b = alloc("S16b", [16, 128], F32, "right")
    g_xs0b, xs0b = alloc("xs0b", [128, T], F32, "right")
    g_xs1b, xs1b = alloc("xs1b", [128, T], F32, "right")
    g_x20, x20 = alloc("x20", [128, T], F32, "right")
    g_x21, x21 = alloc("x21", [128, T], F32, "right")
    g_xbb0, xbb0 = alloc("xbb0", [128, T], BF16, "right")
    g_xbb1, xbb1 = alloc("xbb1", [128, T], BF16, "right")
    g_xqq0, xqq0 = alloc("xqq0", [128, T], BF16, "right")
    g_xqq1, xqq1 = alloc("xqq1", [128, T], BF16, "right")
    g_Rb2, Rb2 = alloc("Rb2", [128, T], F32, "right")
    g_MRb2, MRb2 = alloc("MRb2", [128, T], F32, "right")
    g_lsc2, lsc2 = alloc("lsc2", [128, 3 * 512], F32, "right")
    g_h2s, h2s = alloc("h2s", [128, T], F32, "right")
    g_h2s1, h2s1 = alloc("h2s1", [128, T], F32, "right")
    g_habs, habs = alloc("habs", [128, T], F32, "right")
    h2sb = [h2s, h2s1]
    xsb = [xs0b, xs1b]
    x2b = [x20, x21]
    xbb = [xbb0, xbb1]
    xqq = [xqq0, xqq1]
    sbv = modv[:, 24:32]

    for engh, engn in [(Y, "Y"), (V, "V"), (S, "S")]:
        for s_ in tern_streams + ["dc80"]:
            if s_ in p.cnt:
                p.wait(engh, engn, s_, p.cnt[s_])
        p.wait(engh, engn, eG)
        p.wait(engh, engn, eT, tok_p3T)
    with (nc.psum_tensor("S1ps2", [128, T], F32) as s1t,
          nc.psum_tensor("S2ps2", [128, T], F32) as s2t):
        S1ps, S2ps = s1t[:], s2t[:]
        use_tok = {}
        for d in range(NDC):
            buf = d % 2
            dxs = "dgx%d" % buf
            if d == 0:
                p.wait(G, "G", eG)
                p.wait(G, "G", eT, tok_p3T)
            if d >= 2:
                p.wait(G, "G", eV, use_tok[(d - 2, "x2stt")])
                p.wait(G, "G", eS, use_tok[(d - 2, "x2s")])
                p.wait(V, "V", "dsx%d" % buf, use_tok[(d - 2, "st")])
                p.wait(V, "V", eS, use_tok[(d - 2, "x2s")])
            tok_xd = p.dma(G, "G", dxs, xsb[buf], xT[d * 128:(d + 1) * 128, :])
            p.wait(V, "V", dxs, tok_xd)
            p.wait(V, "V", eV, tok_attn)
            p.wait(V, "V", eV)
            p.inc(V.scalar_tensor_tensor(x2b[buf], hA[:, d * T:(d + 1) * T],
                                         gav[:, d:d + 1], xsb[buf],
                                         ALU.mult, ALU.add), eV)
            tok_x2 = p.cnt[eV]
            use_tok[(d, "x2stt")] = tok_x2
            p.wait(Y, "Y", eV, tok_x2)
            use_tok[(d, "st")] = p.dma(Y, "Y", "dsx%d" % buf,
                                       x2tmp[d * 128:(d + 1) * 128, :], x2b[buf])
            p.wait(V, "V", eV, tok_x2)
            if d >= 2:
                p.wait(V, "V", eT, use_tok[(d - 2, "mm")])
                p.wait(S, "S", eT, use_tok[(d - 2, "mm")])
            p.wait(V, "V", eV)
            p.inc(V.tensor_copy(xbb[buf], x2b[buf]), eV)
            use_tok[(d, "x2v")] = p.cnt[eV]
            p.wait(S, "S", eV, tok_x2)
            p.wait(S, "S", eS)
            p.inc(S.activation(xqq[buf], x2b[buf], AF.Square), eS)
            use_tok[(d, "x2s")] = p.cnt[eS]
            p.wait(TE, "T", eV, use_tok[(d, "x2v")])
            p.wait(TE, "T", eS, use_tok[(d, "x2s")])
            for gq in range(4):
                sl = slice(gq * 512, (gq + 1) * 512)
                p.inc(TE.matmul(S1ps[:, sl], onesb, xbb[buf][:, sl],
                                start=(d == 0), stop=(d == NDC - 1)), eT)
                p.inc(TE.matmul(S2ps[:, sl], onesb, xqq[buf][:, sl],
                                start=(d == 0), stop=(d == NDC - 1)), eT)
            use_tok[(d, "mm")] = p.cnt[eT]
        tok_stats2 = p.cnt[eT]
        mub = lsc2[:, 0:512]
        vrb = lsc2[:, 512:1024]
        tmb = lsc2[:, 1024:1536]
        for gq in range(4):
            sl = slice(gq * 512, (gq + 1) * 512)
            p.wait(V, "V", eT, tok_stats2)
            if gq > 0:
                p.wait(V, "V", eS)
                p.wait(V, "V", eV)
            p.wait(V, "V", eV)
            p.inc(V.tensor_scalar(mub, S1ps[:, sl], 1.0 / D, None, ALU.mult), eV)
            tok_mu = p.cnt[eV]
            p.wait(V, "V", eV)
            p.inc(V.tensor_scalar(vrb, S2ps[:, sl], 1.0 / D, None, ALU.mult), eV)
            p.wait(S, "S", eV, tok_mu)
            p.wait(S, "S", eS)
            p.inc(S.activation(MRb2[:, sl], mub, AF.Square), eS)
            tok_mq = p.cnt[eS]
            p.wait(V, "V", eS, tok_mq)
            p.wait(V, "V", eV)
            p.inc(V.tensor_sub(vrb, vrb, MRb2[:, sl]), eV)
            tvv = p.cnt[eV]
            p.wait(S, "S", eV, tvv)
            p.wait(S, "S", eS)
            p.inc(S.activation(tmb, vrb, AF.Sqrt, bias=cstv[:, 0:1]), eS)
            p.wait(V, "V", eS)
            p.wait(V, "V", eV)
            p.inc(V.reciprocal(Rb2[:, sl], tmb), eV)
            p.wait(V, "V", eV)
            p.wait(V, "V", eV)
            p.inc(V.tensor_mul(MRb2[:, sl], mub, Rb2[:, sl]), eV)
        tok_rb2 = p.cnt[eV]
        if KPHASE == 61:
            raise _Cut(nc, lambda: _finish(hA[:, 0:2048], 128, 2048))

    h2tok = {}

    def emit_h2_pass(consume, hread):
        p.wait(Y, "Y", "dsx0")
        p.wait(Y, "Y", "dsx1")
        use2 = {}
        for d in range(NDC):
            buf = d % 2
            dxs = "dx%d" % buf
            if d < 2:
                p.wait(Y, "Y", eV)
                p.wait(Y, "Y", eS)
            if d >= 2:
                p.wait(Y, "Y", eV, use2[d - 2])
            tok_xd = p.dma(Y, "Y", dxs, xsb[buf],
                           x2tmp[d * 128:(d + 1) * 128, :])
            p.wait(V, "V", dxs, tok_xd)
            p.wait(V, "V", eV, tok_rb2)
            if d >= 2:
                for sm_, tk_ in hread(d - 2):
                    p.wait(V, "V", sm_, tk_)
                p.wait(V, "V", eS, h2tok[d - 2])
            p.wait(V, "V", eV)
            p.inc(V.tensor_mul(x2b[buf], xsb[buf], Rb2), eV)
            use2[d] = p.cnt[eV]
            p.wait(V, "V", eV)
            p.inc(V.tensor_sub(x2b[buf], x2b[buf], MRb2), eV)
            tok_x2n = p.cnt[eV]
            p.wait(S, "S", eV, tok_x2n)
            if d >= 2:
                for sm_, tk_ in hread(d - 2):
                    p.wait(S, "S", sm_, tk_)
            p.wait(S, "S", eS)
            p.inc(S.activation(h2sb[buf], x2b[buf], AF.Identity,
                               scale=bvec[:, d:d + 1], bias=sbv[:, d:d + 1]), eS)
            h2tok[d] = p.cnt[eS]
            consume(d)
        return

    abs_tok = {}
    max_tok = {}

    def emit_maxA(d_):
        p.wait(V, "V", eS, abs_tok[d_])
        p.wait(V, "V", eV)
        if d_ == 0:
            p.inc(V.tensor_copy(mabs, habs), eV)
        else:
            p.inc(V.tensor_max(mabs, mabs, habs), eV)
        max_tok[d_] = p.cnt[eV]

    sp_tok = {}

    def consumeA(d):
        if d >= 1:
            emit_maxA(d - 1)
        p.wait(Y, "Y", eS, h2tok[d])
        sp_tok[d] = p.dma(Y, "Y", "dh2%d" % (d % 2),
                          h2sp[d * 128:(d + 1) * 128, :], h2sb[d % 2])
        p.wait(S, "S", eS, h2tok[d])
        if d >= 1:
            p.wait(S, "S", eV, max_tok[d - 1])
        p.wait(S, "S", eS)
        p.inc(S.activation(habs, h2sb[d % 2], AF.Abs), eS)
        abs_tok[d] = p.cnt[eS]

    emit_h2_pass(consumeA, lambda d_: [(eS, abs_tok[d_]),
                                       ("dh2%d" % (d_ % 2), sp_tok[d_])])
    emit_maxA(NDC - 1)
    tok_mabs = p.cnt[eV]
    if KPHASE == 62:
        raise _Cut(nc, lambda: _finish(mabs, 128, 2048))

    def max_dance(mabs_ap, dst_cols, tok_ready, tag, nblk, mtok_cols,
                  pingpong=True):
        import contextlib
        ctx = contextlib.ExitStack()
        tpa = ctx.enter_context(nc.psum_tensor("tpsA" + tag, [128, 128], F32))
        if pingpong:
            tpb = ctx.enter_context(nc.psum_tensor("tpsB" + tag, [128, 128], F32))
            tp2 = [tpa[:], tpb[:]]
        else:
            tp2 = [tpa[:], tpa[:]]
        with ctx:
            red2 = {}
            for b in range(nblk):
                buf = b % 2
                p.wait(TE, "T", eV, tok_ready)
                gd = b - 2 if pingpong else b - 1
                if gd >= 0:
                    p.wait(TE, "T", eV, red2[gd])
                p.inc(TE.transpose(tp2[buf], mabs_ap[:, b * 128:(b + 1) * 128],
                                   ident), eT)
                tok_tp = p.cnt[eT]
                p.wait(V, "V", eT, tok_tp)
                p.wait(V, "V", eV)
                p.inc(V.reduce_max(mtok_cols[:, b:b + 1], tp2[buf], axis=AXX), eV)
                red2[b] = p.cnt[eV]
        p.wait(V, "V", eV)
        p.wait(V, "V", eV)
        p.inc(V.tensor_scalar(dst_cols, mtok_cols, 1e-5, None, ALU.max), eV)
        return p.cnt[eV]

    def dance_to_s16(tok_src_cols, nb, tok_ready, tag, dst=None):
        if dst is None:
            dst = S16
        with nc.psum_tensor("dtp" + tag, [min(nb, 128), 128], F32) as dtp:
            p.wait(TE, "T", eV, tok_ready)
            p.inc(TE.transpose(dtp[:], tok_src_cols, ident), eT)
            tok_t = p.cnt[eT]
            p.wait(V, "V", eT, tok_t)
            p.wait(V, "V", eV)
            p.inc(V.tensor_copy(dst[0:nb, :], dtp[:]), eV)
        return p.cnt[eV]

    def bcast_from_s16(out_psum, nb, tok_s16v, src=None):
        if src is None:
            src = S16
        p.wait(TE, "T", eV, tok_s16v)
        for b in range(nb):
            lhsT = ident[0:16, b:b + 1].broadcast_to([16, 128])
            p.inc(TE.matmul(out_psum[:, b * 128:(b + 1) * 128], lhsT,
                            src[0:16, :], start=True, stop=True), eT)
        return p.cnt[eT]

    tok_m1 = max_dance(mabs, Max1c, tok_mabs, "1", NTC, Mtok)
    p.wait(V, "V", eV, tok_m1)
    p.wait(V, "V", eV)
    p.inc(V.reciprocal(smallw, Max1c), eV)
    p.wait(V, "V", eV)
    p.wait(V, "V", eV)
    p.inc(V.tensor_scalar(smallw, smallw, 127.0, None, ALU.mult), eV)
    tok_s1tok = p.cnt[eV]
    tok_s1s16 = dance_to_s16(smallw, 16, tok_s1tok, "s1")
    if KPHASE == 63:
        raise _Cut(nc, lambda: _finish(S16[0:16, :], 16, 128))

    p.wait(V, "V", eV, tok_mstat)
    p.wait(V, "V", eV)
    p.inc(V.tensor_scalar(smallw, Max1c, 1.0 / 127.0, None, ALU.mult), eV)
    p.wait(V, "V", eV)
    p.inc(V.tensor_scalar(smallw, smallw, mstat[:, 0:1], None, ALU.mult), eV)
    tok_al2 = p.cnt[eV]
    tok_als16 = dance_to_s16(smallw, 16, tok_al2, "al", dst=S16b)
    with nc.psum_tensor("albps", [128, T], F32) as alt:
        alps = alt[:]
        tok_alps = bcast_from_s16(alps, 16, tok_als16, src=S16b)
        p.wait(V, "V", eT, tok_alps)
        p.wait(V, "V", eV)
        p.inc(V.tensor_copy(alphab, alps[:]), eV)
    tok_alphab = p.cnt[eV]

    g_gpt = nc.psum_tensor("gps", [128, 2 * 512], F32)
    gpt = g_gpt.__enter__()
    g_upt = nc.psum_tensor("ups", [128, 2 * 512], F32)
    upt = g_upt.__enter__()

    g_vq, vq = alloc("vq", [128, T], F32, "right")
    g_aq, aq = alloc("aq", [128, T], F32, "right")
    tok_h2q_d = {}
    with nc.psum_tensor("sbps", [128, T], F32) as sbt:
        Sbps = sbt[:]
        tok_sb = bcast_from_s16(Sbps, 16, tok_s1s16)

        vv_tok = {}
        bb_tok = {}

        def emit_sttB(d_):
            p.wait(V, "V", eS, bb_tok[d_])
            p.wait(V, "V", eV, tok_attn)
            p.wait(V, "V", eV)
            p.inc(V.scalar_tensor_tensor(hA[:, d_ * T:(d_ + 1) * T], habs,
                                         g1m[:, 0:1], aq, ALU.mult, ALU.add), eV)
            tok_h2q_d[d_] = p.cnt[eV]

        for d in range(NDC):
            buf = d % 2
            dxs = "dx%d" % buf
            if d < 2:
                p.wait(Y, "Y", eV)
                p.wait(Y, "Y", eS)
            p.wait(Y, "Y", "dh2%d" % buf, sp_tok[d])
            if d >= 2:
                p.wait(Y, "Y", eV, vv_tok[d - 2])
            tok_hd = p.dma(Y, "Y", dxs, xsb[buf],
                           h2sp[d * 128:(d + 1) * 128, :])
            if d >= 1:
                emit_sttB(d - 1)
            p.wait(V, "V", eT, tok_sb)
            p.wait(V, "V", dxs, tok_hd)
            if d >= 1:
                p.wait(V, "V", eS, bb_tok[d - 1])
            p.wait(V, "V", eV)
            p.inc(V.tensor_tensor(vq, xsb[buf], Sbps, ALU.mult), eV)
            vv_tok[d] = p.cnt[eV]
            p.wait(V, "V", eV)
            p.inc(V.tensor_scalar(aq, vq, M64C, -M64C, ALU.add, ALU.add), eV)
            p.wait(S, "S", eV, vv_tok[d])
            if d >= 1:
                p.wait(S, "S", eV, tok_h2q_d[d - 1])
            p.wait(S, "S", eS)
            p.inc(S.add(habs, vq, cstv[:, 1:2]), eS)
            p.wait(S, "S", eS)
            p.inc(S.add(habs, habs, cstv[:, 2:3]), eS)
            bb_tok[d] = p.cnt[eS]
        emit_sttB(NDC - 1)
        tok_h2q = p.cnt[eV]
        tok_pBs = p.cnt[eS]
        if KPHASE == 64:
            raise _Cut(nc, lambda: _finish(hA[:, 0:2048], 128, 2048))
    for g in [g_aq, g_vq]:
        g.__exit__(None, None, None)

    for g in [g_habs, g_h2s1, g_h2s, g_lsc2, g_MRb2, g_Rb2, g_xqq1, g_xqq0,
              g_xbb1, g_xbb0, g_x21, g_x20, g_xs1b, g_xs0b, g_S16b, g_mabs]:
        g.__exit__(None, None, None)
    if KPHASE < 8:
        _finish(hA[:, 0:2048], 128, 2048)
        return nc

    g_gup, gup = alloc("gup", [128, 32 * 512], F32, "right")
    g_gq, gqb = alloc("gq", [128, 32 * 512], BF16, "right")
    g_s2bt, s2bt = alloc("s2bt", [128, 512], F32, "right")
    g_gmbt, gmbt = alloc("gmbt", [128, 512], F32, "right")
    g_mab2, mab2 = alloc("mab2", [128, 512], F32, "right")

    g_tg0, tg0 = alloc("tg0", [128, 1024], BF16, "left")
    g_tg1, tg1 = alloc("tg1", [128, 1024], BF16, "left")
    g_tg2, tg2 = alloc("tg2", [128, 1024], BF16, "left")
    g_tu0, tu0 = alloc("tu0", [128, 1024], BF16, "left")
    g_tu1, tu1 = alloc("tu1", [128, 1024], BF16, "left")
    g_tu2, tu2 = alloc("tu2", [128, 1024], BF16, "left")
    g_gpb, gpb = alloc("gpb", [128, 512], F32, "right")
    g_sgb, sgb = alloc("sgb", [128, 512], F32, "right")
    g_vq2, vq2 = alloc("vq2", [128, 2 * 512], F32, "right")
    g_aq2, aq2 = alloc("aq2", [128, 2 * 512], BF16, "right")
    g_bq2, bq2 = alloc("bq2", [128, 2 * 512], F32, "right")
    vq2b = [vq2[:, 0:512], vq2[:, 512:1024]]
    aq2b = [aq2[:, 0:512], aq2[:, 512:1024]]
    bq2b = [bq2[:, 0:512], bq2[:, 512:1024]]
    g_td0, td0 = alloc("td0", [128, 4096], BF16, "right")
    g_td1, td1 = alloc("td1", [128, 4096], BF16, "right")
    g_td2, td2 = alloc("td2", [128, 4096], BF16, "right")
    g_x2g0, x2g0 = alloc("x2g0", [128, 512], F32, "right")
    g_x2g1, x2g1 = alloc("x2g1", [128, 512], F32, "right")
    g_ob0, ob0 = alloc("ob0", [128, 512], F32, "right")
    g_ob1, ob1 = alloc("ob1", [128, 512], F32, "right")
    tgc = [tg0, tg1, tg2]
    tuc = [tu0, tu1, tu2]
    tdc = [td0, td1, td2]
    x2g = [x2g0, x2g1]
    outb = [ob0, ob1]


    with (nc.psum_tensor("bc2", [128, 512], F32) as bc2t,
          nc.psum_tensor("dps", [128, 2 * 512], F32) as dpt):
        gps = [gpt[:][:, 0:512], gpt[:][:, 512:1024]]
        ups = [upt[:][:, 0:512], upt[:][:, 512:1024]]
        bc2 = bc2t[:]
        dps = [dpt[:][:, 0:512], dpt[:][:, 512:1024]]
        wread_tok = {}
        gevac = {}
        uevac = {}
        devac = {}
        qv_tok, qa_tok, qb_tok, qs_tok = {}, {}, {}, {}
        bc2_read = 0
        for tg in range(4):
            ts0 = tg * 512
            for oc in range(32):
                buf = oc % 3
                kk = tg * 32 + oc
                dfs = "dgf%d" % buf
                if kk == 0:
                    p.wait(G, "G", eV, tok_alphab)
                if kk >= 3:
                    p.wait(G, "G", eT, wread_tok[kk - 3])
                for s_ in tern_streams:
                    if s_[2] in "gu":
                        p.wait(G, "G", s_, tok_terns[s_])
                p.dma(G, "G", dfs, tgc[buf], terng[:, oc * 1024:(oc + 1) * 1024])
                tok_wd = p.dma(G, "G", dfs, tuc[buf],
                               ternu[:, oc * 1024:(oc + 1) * 1024])
                p.wait(TE, "T", dfs, tok_wd)
                sem_t, tok_t = gevac.get((tg, oc - 2, "g"), (eV, 0))
                p.wait(TE, "T", sem_t, tok_t)
                sem_t, tok_t = uevac.get((tg, oc - 2, "u"), (eV, 0))
                p.wait(TE, "T", sem_t, tok_t)
                pb = oc % 2
                for d in range(NDC):
                    if tg == 0 and oc == 0:
                        p.wait(TE, "T", eV, tok_h2q_d[d])
                    p.inc(TE.matmul(gps[pb], tgc[buf][:, d * 128:(d + 1) * 128],
                                    hA[:, d * T + ts0: d * T + ts0 + 512],
                                    start=(d == 0), stop=(d == NDC - 1)), eT)
                for d in range(NDC):
                    p.inc(TE.matmul(ups[pb], tuc[buf][:, d * 128:(d + 1) * 128],
                                    hA[:, d * T + ts0: d * T + ts0 + 512],
                                    start=(d == 0), stop=(d == NDC - 1)), eT)
                tok_gu = p.cnt[eT]
                wread_tok[kk] = tok_gu
                p.wait(V, "V", eT, tok_gu)
                p.wait(V, "V", eV, tok_alphab)
                if kk > 0:
                    p.wait(V, "V", eS)
                p.wait(V, "V", eV)
                p.inc(V.tensor_tensor(gpb, gps[pb], alphab[:, ts0:ts0 + 512],
                                      ALU.mult), eV)
                tok_gp = p.cnt[eV]
                gevac[(tg, oc, "g")] = (eV, tok_gp)
                p.wait(S, "S", eV, tok_gp)
                if kk > 0:
                    p.wait(S, "S", eV)
                p.wait(S, "S", eS)
                p.inc(S.activation(sgb, gpb, AF.Silu), eS)
                tok_sg = p.cnt[eS]
                p.wait(V, "V", eS, tok_sg)
                gslice = gup[:, oc * 512:(oc + 1) * 512]
                p.wait(V, "V", eV)
                p.inc(V.tensor_tensor(gslice, sgb, ups[pb], ALU.mult), eV)
                tok_gm = p.cnt[eV]
                uevac[(tg, oc, "u")] = (eV, tok_gm)
                p.wait(S, "S", eV, tok_gm)
                p.wait(S, "S", eS)
                p.inc(S.activation(sgb, gslice, AF.Abs), eS)
                tok_ab = p.cnt[eS]
                p.wait(V, "V", eS, tok_ab)
                if oc == 0:
                    p.wait(V, "V", eV)
                    p.inc(V.tensor_copy(mab2, sgb), eV)
                else:
                    p.wait(V, "V", eV)
                    p.inc(V.tensor_max(mab2, mab2, sgb), eV)
            tok_guptg = p.cnt[eV]
            tok_m2 = max_dance(mab2, Max2c[:, tg * 4:(tg + 1) * 4], tok_guptg,
                               "2_%d" % tg, 4, Mtok[:, tg * 4:(tg + 1) * 4],
                               pingpong=False)
            p.wait(V, "V", eV, tok_m2)
            p.wait(V, "V", eV)
            p.inc(V.reciprocal(smallw[:, 0:4], Max2c[:, tg * 4:(tg + 1) * 4]), eV)
            p.wait(V, "V", eV)
            p.wait(V, "V", eV)
            p.inc(V.tensor_scalar(smallw[:, 0:4], smallw[:, 0:4], 127.0,
                                  None, ALU.mult), eV)
            p.wait(V, "V", eV)
            p.inc(V.tensor_mul(smallw[:, 4:8], Max1c[:, tg * 4:(tg + 1) * 4],
                               Max2c[:, tg * 4:(tg + 1) * 4]), eV)
            p.wait(V, "V", eV)
            p.wait(V, "V", eV)
            p.inc(V.tensor_scalar(smallw[:, 4:8], smallw[:, 4:8],
                                  mstat[:, 2:3], None, ALU.mult), eV)
            p.wait(V, "V", eV)
            p.wait(V, "V", eV)
            p.inc(V.tensor_scalar(smallw[:, 4:8], smallw[:, 4:8],
                                  mstat[:, 4:5], None, ALU.mult), eV)
            p.wait(V, "V", eV)
            p.wait(V, "V", eV)
            p.inc(V.tensor_scalar(smallw[:, 4:8], smallw[:, 4:8],
                                  1.0 / (127.0 * 127.0), None, ALU.mult), eV)
            tok_sg2 = p.cnt[eV]
            tok_d16 = dance_to_s16(smallw[:, 0:8], 8, tok_sg2, "sg%d" % tg)
            p.wait(TE, "T", eV, bc2_read)
            p.wait(TE, "T", eV, tok_d16)
            for bb in range(4):
                lhsT = ident[0:16, bb:bb + 1].broadcast_to([16, 128])
                p.inc(TE.matmul(bc2[:, bb * 128:(bb + 1) * 128],
                                lhsT, S16[0:16, :], start=True, stop=True), eT)
            tok_b1 = p.cnt[eT]
            p.wait(V, "V", eT, tok_b1)
            p.wait(V, "V", eV)
            p.inc(V.tensor_copy(s2bt, bc2), eV)
            bc2_read = p.cnt[eV]
            p.wait(TE, "T", eV, bc2_read)
            for bb in range(4):
                lhsT = ident[0:16, 4 + bb:5 + bb].broadcast_to([16, 128])
                p.inc(TE.matmul(bc2[:, bb * 128:(bb + 1) * 128], lhsT,
                                S16[0:16, :], start=True, stop=True), eT)
            tok_b2 = p.cnt[eT]
            p.wait(V, "V", eT, tok_b2)
            p.wait(V, "V", eV)
            p.inc(V.tensor_copy(gmbt, bc2), eV)
            bc2_read = p.cnt[eV]
            tok_scales = p.cnt[eV]
            dtoks = {}
            for d_ in (0, 1, 2):
                kkd = tg * 8 + d_
                if tg == 0:
                    p.wait(G, "G", eV, tok_h2q)
                    p.wait(G, "G", eS, tok_pBs)
                if kkd >= 3:
                    p.wait(G, "G", eT, devac.get((kkd - 3, "mm"), 0))
                for s_ in tern_streams:
                    if s_[2] == "d":
                        p.wait(G, "G", s_, tok_terns[s_])
                dtoks[d_] = p.dma(G, "G", "dd%d" % (d_ % 3), tdc[d_ % 3],
                                  ternd[:, d_ * 4096:(d_ + 1) * 4096])

            def emit_qstt(oc_):
                b2_ = oc_ % 2
                qk_ = tg * 32 + oc_
                p.wait(V, "V", eS, qb_tok[qk_])
                p.wait(V, "V", eV)
                p.inc(V.scalar_tensor_tensor(gqb[:, oc_ * 512:(oc_ + 1) * 512],
                                             bq2b[b2_], g1m[:, 0:1], aq2b[b2_],
                                             ALU.mult, ALU.add), eV)
                qs_tok[qk_] = p.cnt[eV]

            def emit_dmm(oc_):
                for d_ in (0, 1):
                    kkd = tg * 8 + d_
                    if oc_ == 0:
                        p.wait(TE, "T", "dd%d" % d_, dtoks[d_])
                        if kkd >= 2:
                            p.wait(TE, "T", eV, devac.get((kkd - 2, "ev"), 0))
                    p.wait(TE, "T", eV, qs_tok[tg * 32 + oc_])
                    p.inc(TE.matmul(dps[d_], tdc[d_][:, oc_ * 128:(oc_ + 1) * 128],
                                    gqb[:, oc_ * 512:(oc_ + 1) * 512],
                                    start=(oc_ == 0), stop=(oc_ == 31)), eT)
                if oc_ == 31:
                    for d_ in (0, 1):
                        devac[(tg * 8 + d_, "mm")] = p.cnt[eT]
            for oc in range(32):
                b2 = oc % 2
                qk = tg * 32 + oc
                gsl = gup[:, oc * 512:(oc + 1) * 512]
                p.wait(V, "V", eV, tok_scales)
                if qk >= 2:
                    p.wait(V, "V", eS, qb_tok[qk - 2])
                p.wait(V, "V", eV)
                p.inc(V.tensor_tensor(vq2b[b2], gsl, s2bt, ALU.mult), eV)
                qv_tok[qk] = p.cnt[eV]
                p.wait(V, "V", eV)
                p.inc(V.tensor_scalar(aq2b[b2], vq2b[b2], M64C, -M64C,
                                      ALU.add, ALU.add), eV)
                qa_tok[qk] = p.cnt[eV]
                p.wait(S, "S", eV, qv_tok[qk])
                if qk >= 2:
                    p.wait(S, "S", eV, qs_tok[qk - 2])
                p.wait(S, "S", eS)
                p.inc(S.add(bq2b[b2], vq2b[b2], cstv[:, 1:2]), eS)
                p.wait(S, "S", eS)
                p.inc(S.add(bq2b[b2], bq2b[b2], cstv[:, 2:3]), eS)
                qb_tok[qk] = p.cnt[eS]
                if oc >= 1:
                    emit_qstt(oc - 1)
                    emit_dmm(oc - 1)
            emit_qstt(31)
            emit_dmm(31)
            tok_gqtg = p.cnt[eV]
            for d in range(NDC):
                buf = d % 2
                wb = d % 3
                kk = tg * 8 + d
                dxr = "dxr%d" % buf
                if d >= 3:
                    dds = "dd%d" % wb
                    p.wait(G, "G", eT, devac.get((kk - 3, "mm"), 0))
                    tok_wdd = p.dma(G, "G", dds, tdc[wb],
                                    ternd[:, d * 4096:(d + 1) * 4096])
                else:
                    tok_wdd = dtoks.get(d, 0)
                if tg == 0 and d == 0:
                    p.wait(Y, "Y", eV, tok_h2q)
                    p.wait(Y, "Y", eS, tok_pBs)
                if kk >= 2:
                    p.wait(Y, "Y", eV, devac.get((kk - 2, "ev"), 0))
                tok_x2d = p.dma(Y, "Y", dxr, x2g[buf],
                                x2tmp[d * 128:(d + 1) * 128, ts0:ts0 + 512])
                if d >= 2:
                    p.wait(TE, "T", "dd%d" % wb, tok_wdd)
                    p.wait(TE, "T", eV, tok_gqtg)
                    p.wait(TE, "T", eV, devac.get((kk - 2, "ev"), 0))
                    for oc in range(32):
                        p.inc(TE.matmul(dps[buf],
                                        tdc[wb][:, oc * 128:(oc + 1) * 128],
                                        gqb[:, oc * 512:(oc + 1) * 512],
                                        start=(oc == 0), stop=(oc == 31)), eT)
                    devac[(kk, "mm")] = p.cnt[eT]
                tok_dmm = devac[(kk, "mm")]
                p.wait(V, "V", eT, tok_dmm)
                p.wait(V, "V", dxr, tok_x2d)
                p.wait(V, "V", eV)
                if kk >= 2:
                    p.wait(V, "V", "dso%d" % buf, devac.get((kk - 2, "st"), 0))
                p.wait(V, "V", eV)
                p.inc(V.tensor_tensor(outb[buf], dps[buf], gmbt, ALU.mult), eV)
                p.wait(V, "V", eV)
                p.wait(V, "V", eV)
                p.inc(V.scalar_tensor_tensor(outb[buf], outb[buf], gbv[:, d:d + 1],
                                             x2g[buf], ALU.mult, ALU.add), eV)
                devac[(kk, "ev")] = p.cnt[eV]
                p.wait(Y, "Y", eV, devac[(kk, "ev")])
                devac[(kk, "st")] = p.dma(
                    Y, "Y", "dso%d" % buf,
                    outT[d * 128:(d + 1) * 128, ts0:ts0 + 512], outb[buf])
    g_upt.__exit__(None, None, None)
    g_gpt.__exit__(None, None, None)
    p.wait(Y, "Y", "dso0")
    p.wait(Y, "Y", "dso1")

    for g in [g_ob1, g_ob0, g_x2g1, g_x2g0, g_td2, g_td1, g_td0, g_bq2, g_aq2, g_vq2,
              g_sgb, g_gpb, g_tu2, g_tu1, g_tu0, g_tg2, g_tg1, g_tg0, g_mab2,
              g_gmbt, g_s2bt, g_gq, g_gup, g_alb]:
        g.__exit__(None, None, None)

    for g in reversed(pg):
        g.__exit__(None, None, None)
    blkg.__exit__(None, None, None)
    for g in reversed(p.guards):
        g.__exit__(None, None, None)
    return nc


_cache = {}


def kernel(L0_attn_z, L0_attn_W_qkv, L0_ffn_w_gate, L0_ffn_w_up, L0_ffn_w_down,
           L0_c, L0_wmod):
    import ml_dtypes
    B = L0_attn_z.shape[0]
    Wq = np.asarray(L0_attn_W_qkv, np.float32).reshape(NH, 3, DH, D)
    Wre = np.concatenate([Wq[:, 0].reshape(D, D), Wq[:, 1].reshape(D, D),
                          Wq[:, 2].reshape(D, D)], axis=0)
    wqkvT = np.ascontiguousarray(Wre.T)
    wgT = np.ascontiguousarray(np.asarray(L0_ffn_w_gate, np.float32).T)
    wuT = np.ascontiguousarray(np.asarray(L0_ffn_w_up, np.float32).T)
    wdT = np.ascontiguousarray(np.asarray(L0_ffn_w_down, np.float32).T)
    wmodc = np.ascontiguousarray(np.asarray(L0_wmod, np.float32))
    ones128b = np.ones((128, 128), ml_dtypes.bfloat16)
    bd = np.zeros((128, 128), ml_dtypes.bfloat16)
    bd[:64, :64] = 1
    bd[64:, 64:] = 1
    ones128f = np.ones((128, 128), np.float32)
    onescolf = np.ones((1, 128), np.float32)
    onecolbf = np.ones((128, 1), ml_dtypes.bfloat16)
    identf = np.eye(128, dtype=np.float32)
    cstv = np.tile(np.array([[EPS, M32C, -M32C, 0.0]], np.float32), (128, 1))

    if "nc" not in _cache:
        _cache["nc"] = build_nc()
    nc = _cache["nc"]

    in_maps = []
    for b in range(B):
        in_maps.append({
            "xT": np.ascontiguousarray(np.asarray(L0_attn_z[b], np.float32).T),
            "cT": np.ascontiguousarray(np.asarray(L0_c[b], np.float32).reshape(D, 1)),
            "wqkvT": wqkvT, "wmod": wmodc, "wgT": wgT, "wuT": wuT, "wdT": wdT,
            "ones128b": ones128b, "bdiagb": bd, "ones128f": ones128f,
            "onescolf": onescolf, "onecolbf": onecolbf, "identf": identf,
            "cstv": cstv,
        })
    res = run_bass_kernel_spmd(nc, in_maps, core_ids=list(range(B)))
    out = np.stack([np.ascontiguousarray(np.asarray(r["outT"]).T)
                    for r in res.results])
    return out.astype(np.float32)
